# version 09 (final)
# baseline (speedup 1.0000x reference)
import numpy as np
import concourse.bacc as bacc
import concourse.mybir as mybir
from concourse.library_config import mlp as LIB_MLP, local_scatter as LIB_LS

F32 = mybir.dt.float32
BF16 = mybir.dt.bfloat16
I16 = mybir.dt.int16
ALU = mybir.AluOpType
ACT = mybir.ActivationFunctionType
AXX = mybir.AxisListType.X

D = 2048
CAP = 1024
WIN = 9
ATTN_SCALE = 0.08838834764831845
LN_ATTN = float(np.log(ATTN_SCALE))
RG = [list(range(8))]


_ISZ = {"float32": 4, "bfloat16": 2, "int16": 2, "int32": 4, "uint8": 1,
        "float16": 2, "uint16": 2, "uint32": 4}


def _ext(ap):
    try:
        isz = _ISZ[str(ap.dtype).split(".")[-1]]
        span = 1
        pat = list(ap.ap)
        for pr in pat[1:]:
            span += (pr[1] - 1) * abs(pr[0])
        lo = ap.offset * isz
        return (ap.name, lo, lo + span * isz)
    except Exception:
        return None


class Prog:
    def __init__(self, nc):
        self.nc = nc
        self.ops = []
        self.cnt = {}
        self.semh = {}
        self.qcnt = {}
        self.cnt_play = {}

    def sems(self, names):
        for n in names:
            self.cnt[n] = 0

    def emit(self, eng, fn, waits=(), inc=None, amt=1, rd=None, wr=None):
        ws = tuple((s, v) for (s, v) in waits if v and v > 0)
        if rd is not None:
            rd = [_ext(a) for a in rd]
            if any(x is None for x in rd):
                rd = None
        if wr is not None:
            wr = [_ext(a) for a in wr]
            if any(x is None for x in wr):
                wr = None
        self.ops.append((eng, fn, ws, inc, amt, rd, wr))
        if inc is not None:
            self.cnt[inc] += amt
            return self.cnt[inc]
        return None

    NL = 8

    def _lane(self, eng):
        base = {"sync": "dS", "scalar": "dC", "gpsimd": "dG"}[eng]
        k = self.qcnt.get(eng, 0)
        self.qcnt[eng] = k + 1
        sem = "%s%d" % (base, k % self.NL)
        if sem not in self.cnt:
            self.cnt[sem] = 0
        return sem

    def dma(self, eng, out, in_, waits=()):
        sem = self._lane(eng)
        ws = list(waits) + ([(sem, self.cnt[sem])] if self.cnt[sem] else [])
        v = self.emit(eng, (lambda e, o=out, i=in_: e.dma_start(out=o, in_=i)),
                      waits=ws, inc=sem, amt=16)
        return (sem, v)

    def dmalike(self, eng, fn, waits=(), base=None):
        if base is None:
            sem = self._lane(eng)
        else:
            k = self.qcnt.get(base, 0)
            self.qcnt[base] = k + 1
            sem = "%s%d" % (base, k % self.NL)
            if sem not in self.cnt:
                self.cnt[sem] = 0
        ws = list(waits) + ([(sem, self.cnt[sem])] if self.cnt[sem] else [])
        v = self.emit(eng, fn, waits=ws, inc=sem, amt=16)
        return (sem, v)

    def qbar(self, eng):
        base = {"sync": "dS", "scalar": "dC", "gpsimd": "dG"}[eng]
        return [(sem, v) for sem, v in self.cnt.items()
                if sem.startswith(base) and v > 0]

    def val(self, sem):
        return self.cnt[sem]

    def play(self, eng_name, eng):
        own = {"vector": "V", "scalar": "S", "gpsimd": "G"}.get(eng_name)
        ew = {}
        by_name = {}
        last_all = 0
        myval = 0
        for (e, fn, waits, inc, amt, rd, wr) in self.ops:
            if e != eng_name:
                continue
            if own is not None and inc == own:
                if rd is None:
                    need = myval
                else:
                    need = last_all
                    for (name, lo, hi) in rd:
                        for (val, l2, h2) in reversed(by_name.get(name, ())):
                            if val <= need:
                                break
                            if l2 < hi and lo < h2:
                                need = val
                                break
                if need > 0 and ew.get(own, 0) < need:
                    eng.wait_ge(self.semh[own], need)
                    ew[own] = need
            for (s, v) in waits:
                if ew.get(s, 0) < v:
                    eng.wait_ge(self.semh[s], v)
                    ew[s] = v
            inst = fn(eng)
            if inc is not None:
                inst.then_inc(self.semh[inc], amt)
                if own is not None and inc == own:
                    myval += amt
                    if wr is None:
                        last_all = myval
                    else:
                        for (name, lo, hi) in wr:
                            by_name.setdefault(name, []).append((myval, lo, hi))


def build_nc(stage=9, debug=0):
    nc = bacc.Bacc("TRN2", num_swdge_queues=2)

    def din(name, shape, dt):
        return nc.dram_tensor(name, shape, dt, kind="ExternalInput").ap()

    def dout(name, shape, dt):
        return nc.dram_tensor(name, shape, dt, kind="ExternalOutput").ap()

    def dint(name, shape, dt):
        return nc.dram_tensor(name, shape, dt, kind="Internal").ap()

    x_own = din("x_own", [1024, D], F32)
    x_halo = din("x_halo", [1024, D], F32)
    wq = din("wq", [D, 2048], F32)
    wkv = din("wkv", [D, 2048], F32)
    wo = din("wo", [16, 128, D], F32)
    gate_wT = din("gate_wT", [D, 32], F32)
    up_wT = din("up_wT", [4, D, 1024], F32)
    down_wT = din("down_wT", [4, 1024, D], F32)
    sh_upT = din("sh_upT", [D, 4096], F32)
    sh_dnT = din("sh_dnT", [4096, D], F32)
    qns = din("qns", [128, 128], BF16)
    kns = din("kns", [128, 128], BF16)
    brow_in = din("brow_in", [1, 256], F32)
    actc = din("actc", [128, 8], F32)
    sinq = din("sinq", [128, 8, 64], BF16)
    cosq = din("cosq", [128, 8, 64], BF16)
    sink = din("sink", [128, 16, 64], BF16)
    cosk = din("cosk", [128, 16, 64], BF16)
    maskin = din("maskin", [8, 128, WIN * 128], BF16)
    tri_diag = din("tri_diag", [128, 128], BF16)
    tri_far = din("tri_far", [128, 128], BF16)
    tri_cum = din("tri_cum", [128, 128], BF16)
    identb = din("identb", [128, 128], BF16)
    identf = din("identf", [128, 128], F32)
    i4f = din("i4f", [128, 4], F32)
    i4b = din("i4b", [128, 4], BF16)
    ones_f = din("ones_f", [1, 128], F32)
    ones_cb = din("ones_cb", [128, 1], BF16)

    out = dout("out", [1024, D], F32)

    ag_sc_in = dint("ag_sc_in", [1024, 32], F32)
    scores_all = dint("scores_all", [8192, 32], F32)
    ag_xn_in = dint("ag_xn_in", [1024, D], BF16)
    xn_all = dint("xn_all", [8192, D], BF16)
    st_dump = dint("st_dump", [128, 4, 64], I16)
    whi_dump = dint("whi_dump", [128, 4, 64], BF16)
    wlo_dump = dint("wlo_dump", [128, 4, 64], BF16)
    gidx_dump = dint("gidx_dump", [4, CAP], I16)
    sidx_dump = dint("sidx_dump", [4, CAP], I16)
    partial = dint("partial", [8320, D], BF16)
    routed = dint("routed", [1024, D], BF16)
    s_dram = dint("s_dram", [1024, D], BF16)
    wq_bf = dint("wq_bf", [D, 2048], BF16)
    wkv_bf = dint("wkv_bf", [D, 2048], BF16)
    wo_bf = dint("wo_bf", [16, 128, D], BF16)
    shup_bf = dint("shup_bf", [D, 4096], BF16)
    shdn_bf = dint("shdn_bf", [4096, D], BF16)

    dbg = {}
    if debug:
        dbg["xn_all_o"] = dout("xn_all_o", [8192, D], BF16)
        dbg["scores_o"] = dout("scores_o", [8192, 32], F32)
        dbg["gidx_o"] = dout("gidx_o", [4, CAP], I16)
        dbg["sidx_o"] = dout("sidx_o", [4, CAP], I16)
        dbg["wslot_o"] = dout("wslot_o", [128, 4, 8], F32)
        dbg["routed_o"] = dout("routed_o", [1024, D], BF16)
        dbg["s_o"] = dout("s_o", [1024, D], BF16)
        dbg["shared_o"] = dout("shared_o", [1024, D], F32)
        dbg["sin_o"] = dout("sin_o", [1024, D], BF16)
        dbg["rtd_o"] = dout("rtd_o", [1024, D], BF16)

    stack = []

    def sb(name, shape, dt):
        t = nc.sbuf_tensor(name, shape, dt)
        stack.append(t)
        return t.__enter__()

    def psum(name, shape, dt=F32):
        t = nc.psum_tensor(name, shape, dt)
        stack.append(t)
        return t.__enter__()

    xnT = sb("xnT", [128, 16, 1024], BF16)
    AR1 = sb("AR1", [128, 24576], BF16)
    AR2 = sb("AR2", [128, 8192], BF16)
    AR3 = sb("AR3", [128, 18432], BF16)
    wst = sb("wst", [128, 16, 512], BF16)
    ln_x = sb("ln_x", [128, 2, D], F32)
    xn_rows = sb("xn_rows", [128, 2, D], BF16)
    rows_a = sb("rows_a", [128, 1024], F32)
    rows_b = sb("rows_b", [128, 1024], F32)
    rms_t = sb("rms_t", [128, 1024], F32)
    rms_s = sb("rms_s", [128, 16], F32)
    ln_stat = sb("ln_stat", [128, 32], F32)
    rope_q = sb("rope_q", [128, 2, 8, 64], BF16)
    rope_k = sb("rope_k", [128, 2, 16, 64], BF16)
    scale_q = sb("scale_q", [128, 128], BF16)
    scale_k = sb("scale_k", [128, 128], BF16)
    identb_sb = sb("identb_sb", [128, 128], BF16)
    identf_sb = sb("identf_sb", [128, 128], F32)
    tri_cum_sb = sb("tri_cum_sb", [128, 128], BF16)
    tri_diag_sb = sb("tri_diag_sb", [128, 128], BF16)
    tri_far_sb = sb("tri_far_sb", [128, 128], BF16)
    ones_f_sb = sb("ones_f_sb", [1, 128], F32)
    ones_cb_sb = sb("ones_cb_sb", [128, 1], BF16)
    i4f_sb = sb("i4f_sb", [128, 4], F32)
    i4b_sb = sb("i4b_sb", [128, 4], BF16)

    brow_sb = sb("brow_sb", [1, 256], F32)
    actc_sb = sb("actc_sb", [128, 8], F32)
    gwT_sb = sb("gwT_sb", [128, 16, 32], BF16)
    bias_t = sb("bias_t", [128, 32], F32)
    epse_t = sb("epse_t", [128, 32], F32)
    esel_t = sb("esel_t", [128, 4, 32], F32)
    m_max = sb("m_max", [128, 64], F32)
    wsum = sb("wsum", [128, 64], F32)
    wslot = sb("wslot", [128, 4, 8], F32)
    gidx_rep = sb("gidx_rep", [128, CAP // 16], I16)
    sidx_rep = sb("sidx_rep", [128, CAP // 16], I16)
    negmax = sb("negmax", [128, 2], F32)
    dsum = sb("dsum", [128, 2], F32)
    s_stage = sb("s_stage", [128, 2, D], BF16)

    qT = AR1[:, 0:8192].rearrange("p (a t) -> p a t", a=8)
    kT = AR1[:, 8192:16384].rearrange("p (a t) -> p a t", a=4)
    v_sb = AR1[:, 16384:24576].rearrange("p (t a h) -> p t a h", t=16, a=4)
    halo_ring = AR2[:, 0:8192].rearrange("p (c t) -> p c t", c=16)
    probs = AR2[:, 0:2304].rearrange("p (a t) -> p a t", a=2)
    maskt = AR2[:, 2304:3456]
    encT = AR2[:, 3456:5504].rearrange("p (a b h) -> p a b h", a=8, b=2)
    wo_sb = AR2[:, 5504:6528].rearrange("p (a h) -> p a h", a=2)
    probsT = AR2[:, 6528:7680].rearrange("p (a h) -> p a h", a=WIN)
    AR3f = AR3.bitcast(F32)
    sc_all = AR3f[:, 0:2048].rearrange("p (t e) -> p t e", t=64)
    sel_t = AR3f[:, 2048:4096].rearrange("p (t e) -> p t e", t=64)
    selr = AR3f[:, 4096:6144].rearrange("p (t e) -> p t e", t=64)
    pos_t = AR3f[:, 6144:8192].rearrange("p (t e) -> p t e", t=64)
    sel_bf = AR3[:, 16384:18432].rearrange("p (t e) -> p t e", t=64)
    xeT2 = [AR1[:, 0:8192].rearrange("p (c t) -> p c t", c=16),
            AR1[:, 8192:16384].rearrange("p (c t) -> p c t", c=16)]
    hT = AR1[:, 16384:24576].rearrange("p (j t) -> p j t", j=8)
    yew = AR3[:, 0:16384].rearrange("p (a d) -> p a d", a=8)
    dw_sb = AR2[:, 0:8192].rearrange("p (b j n) -> p b j n", b=2, j=8)
    h1T = AR1[:, 0:8192].rearrange("p (j t) -> p j t", j=32)
    g_f32 = AR1[:, 8192:16384].bitcast(F32).rearrange("p (a d) -> p a d", a=2)
    xo_f32 = AR1[:, 16384:24576].bitcast(F32).rearrange("p (a d) -> p a d", a=2)
    sg_f32 = AR3.bitcast(F32)[:, 0:4096].rearrange("p (a d) -> p a d", a=2)
    routed_sb = AR3[:, 8192:12288].rearrange("p (a d) -> p a d", a=2)
    s_in = AR3[:, 12288:16384].rearrange("p (a d) -> p a d", a=2)
    shw_up = AR2[:, 0:4096].rearrange("p (b c h) -> p b c h", b=2, c=16)
    shw_dn = AR2[:, 4096:8192].rearrange("p (b n) -> p b n", b=2)
    zero_v = AR1[:, 0:2048]
    AR2f = AR2.bitcast(F32)
    cnt_t = AR2f[0:1, 0:2048].rearrange("p (t e) -> p t e", t=64)
    off_t = AR2f[0:1, 2048:4096].rearrange("p (t e) -> p t e", t=64)
    sigT = AR3.bitcast(F32)[0:32, 0:1024]
    sc_rows = AR3.bitcast(F32)[:, 1024:1280].rearrange("p (t e) -> p t e", t=8)
    wstf = wst.rearrange("p a b -> p (a b)").bitcast(F32)
    mrg = wstf[0:4, 0:3072]
    xnrf = xn_rows.rearrange("p a d -> p (a d)").bitcast(F32)
    r_a = xnrf[:, 0:512].rearrange("p (t e) -> p t e", t=64)
    r_b = xnrf[:, 512:1024].rearrange("p (t e) -> p t e", t=64)
    r_c = xnrf[:, 1024:1536].rearrange("p (t e) -> p t e", t=64)
    r_d = xnrf[:, 1536:2048].rearrange("p (t e) -> p t e", t=64)
    rank_t = rows_a[:, 0:512].rearrange("p (t e) -> p t e", t=64)
    selg = rows_a[:, 512:1024].rearrange("p (t e) -> p t e", t=64)
    lso_tok_f = ln_x[:, 0, 0:1024]
    own3 = ln_x[:, 1, 0:768].rearrange("p (k t) -> p k t", k=12)
    own_i = ln_x[:, 1, 768:896].bitcast(I16).rearrange("p (e t) -> p e t", e=4)
    own_hi = ln_x[:, 1, 896:1024].bitcast(BF16).rearrange("p (e t) -> p e t", e=4)
    own_lo = ln_x[:, 1, 1024:1152].bitcast(BF16).rearrange("p (e t) -> p e t", e=4)
    relu_f = rows_a[:, 0:512]
    rbi = rows_b.bitcast(I16)
    rbb = rows_b.bitcast(BF16)
    ls_idx = rbi[:, 1024:1280]
    ls_tok = rbi[:, 1280:1536]
    ls_whi = rbb[:, 1536:1792]
    ls_wlo = rbb[:, 1792:2048]
    wsti = wst.rearrange("p a b -> p (a b)").bitcast(I16)
    gidx_i = wsti[0:4, 6144:7168]
    sidx_i = wsti[0:4, 7168:8192]
    lso_tok = rows_b.bitcast(I16)[:, 0:1024]
    lso_whi = rms_t.bitcast(BF16)[:, 0:1024]
    lso_wlo = rms_t.bitcast(BF16)[:, 1024:2048]

    ps_l1 = psum("ps_l1", [128, 1536], F32)
    ps_l2 = psum("ps_l2", [128, 1536], F32)
    ps_a = psum("ps_a", [128, 512], F32)
    ps_b = psum("ps_b", [128, 512], F32)

    p = Prog(nc)
    p.sems(["T", "V", "S", "G", "CC"])

    def T(fn, waits=()):
        return p.emit("tensor", fn, waits, "T")

    def V(fn, waits=(), rd=None, wr=None):
        return p.emit("vector", fn, waits, "V", rd=rd, wr=wr)

    def S(fn, waits=(), rd=None, wr=None):
        return p.emit("scalar", fn, waits, "S", rd=rd, wr=wr)

    def G(fn, waits=(), rd=None, wr=None):
        return p.emit("gpsimd", fn, waits, "G", rd=rd, wr=wr)

    def Gd(fn, waits=()):
        return p.dmalike("gpsimd", fn, waits, base="dX")

    def CCe(fn, waits=()):
        return p.emit("gpsimd", fn, waits, "CC", 1)

    def bar(*engs):
        return [(e, p.val(e)) for e in engs]

    for dst, src in [(identb_sb, identb), (identf_sb, identf), (tri_cum_sb, tri_cum),
                     (tri_diag_sb, tri_diag), (tri_far_sb, tri_far),
                     (ones_f_sb, ones_f), (ones_cb_sb, ones_cb), (i4f_sb, i4f),
                     (i4b_sb, i4b), (scale_q, qns),
                     (scale_k, kns), (brow_sb, brow_in), (actc_sb, actc)]:
        p.dma("sync", dst[:, :], src[:, :])
    p.dma("sync", rope_q[:, 0, :, :], sinq[:, :, :])
    p.dma("sync", rope_q[:, 1, :, :], cosq[:, :, :])
    p.dma("sync", rope_k[:, 0, :, :], sink[:, :, :])
    p.dma("sync", rope_k[:, 1, :, :], cosk[:, :, :])
    v_consts = p.qbar("sync")

    if stage < 0.22:
        _finish(p, nc)
        return nc, dbg
    if stage < 0.24:
        _finish(p, nc)
        return nc, dbg
    p.dma("gpsimd", gwT_sb[:, :, :], gate_wT.rearrange("(c p) n -> p c n", p=128))
    v_pc_gw = p.qbar("gpsimd")
    for (d_, s_) in [(wkv_bf, wkv), (wq_bf, wq)]:
        for h in range(2):
            p.dma("gpsimd", d_.rearrange("(a p) n -> a p n", p=128)[h * 8:(h + 1) * 8],
                  s_.rearrange("(a p) n -> a p n", p=128)[h * 8:(h + 1) * 8])
        if d_ is wkv_bf:
            v_pc_kv = p.qbar("gpsimd")
    v_pc_q = p.qbar("gpsimd")
    p.dma("gpsimd", wo_bf.rearrange("n p d -> p n d"), wo.rearrange("n p d -> p n d"))
    v_pc_wo = p.qbar("gpsimd")
    for h in range(2):
        p.dma("gpsimd", shup_bf.rearrange("(a p) n -> a p n", p=128)[h * 8:(h + 1) * 8],
              sh_upT.rearrange("(a p) n -> a p n", p=128)[h * 8:(h + 1) * 8])
    v_pc_shup = p.qbar("gpsimd")
    for h in range(4):
        p.dma("gpsimd", shdn_bf.rearrange("(a p) n -> a p n", p=128)[h * 8:(h + 1) * 8],
              sh_dnT.rearrange("(a p) n -> a p n", p=128)[h * 8:(h + 1) * 8])
    v_pc_shdn = p.qbar("gpsimd")
    v_precast = p.qbar("gpsimd")

    if stage < 0.25:
        _finish(p, nc)
        return nc, dbg
    vz = G(lambda g: g.memset(zero_v, 0.0), wr=[zero_v])

    if stage < 0.3:
        _finish(p, nc)
        return nc, dbg
    vb = T(lambda t: t.matmul(ps_b[:, 0:256], ones_f_sb[:1, :], brow_sb[:1, :],
                              start=True, stop=True), waits=v_consts)
    V(lambda v: v.tensor_copy(bias_t[:, :], ps_b[:, 0:32]), waits=[("T", vb)],
      rd=[ps_b[:, 0:32]], wr=[bias_t[:, :]])
    V(lambda v: v.tensor_copy(epse_t[:, :], ps_b[:, 32:64]),
      rd=[ps_b[:, 32:64]], wr=[epse_t[:, :]])
    v_bias = V(lambda v: v.tensor_copy(esel_t.rearrange("p a e -> p (a e)"),
                                       ps_b[:, 64:192]),
               rd=[ps_b[:, 64:192]], wr=[esel_t[:, :, :]])

    if stage < 0.4:
        _finish(p, nc)
        return nc, dbg
    LNE = 1e-5
    ln_guard = [0, 0]
    xnr_guard = [[], []]

    ln_dguard = [[]]

    def emit_ln(src_rows, slot):
        vw = p.dma("sync", ln_x[:, slot, :], src_rows,
                   waits=[("V", ln_guard[slot])] + ln_dguard[0])
        vbn = 0
        for j in range(4):
            vbn = V(lambda v, j=j, slot=slot: v.bn_stats(ln_stat[:, j * 6:(j + 1) * 6],
                                                         ln_x[:, slot, j * 512:(j + 1) * 512]),
                    waits=[vw] if j == 0 else (),
                    rd=[ln_x[:, slot, j * 512:(j + 1) * 512]],
                    wr=[ln_stat[:, j * 6:(j + 1) * 6]])
        vag = V(lambda v: v.bn_aggr(ln_stat[:, 24:26], ln_stat[:, 0:24]),
                waits=[("V", vbn)],
                rd=[ln_stat[:, 0:24]], wr=[ln_stat[:, 24:26]])
        vv = V(lambda v: v.tensor_scalar(ln_stat[:, 26:27], ln_stat[:, 25:26], LNE,
                                         None, ALU.add), waits=[("V", vag)],
               rd=[ln_stat[:, 25:26]], wr=[ln_stat[:, 26:27]])
        S(lambda s: s.activation(ln_stat[:, 27:28], ln_stat[:, 26:27], ACT.Ln),
          waits=[("V", vv)] + v_consts,
          rd=[ln_stat[:, 26:27]], wr=[ln_stat[:, 27:28]])
        vs = S(lambda s: s.activation(ln_stat[:, 28:29], ln_stat[:, 27:28], ACT.Exp,
                                      scale=actc_sb[:, 2:3]),
               rd=[ln_stat[:, 27:28], actc_sb[:, 2:3]], wr=[ln_stat[:, 28:29]])
        vout = V(lambda v, slot=slot: v.tensor_scalar(xn_rows[:, slot, :],
                                                      ln_x[:, slot, :],
                                                      ln_stat[:, 24:25], ln_stat[:, 28:29],
                                                      ALU.subtract, ALU.mult),
                 waits=[("S", vs)] + xnr_guard[slot],
                 rd=[ln_x[:, slot, :], ln_stat[:, 24:25], ln_stat[:, 28:29]],
                 wr=[xn_rows[:, slot, :]])
        ln_guard[slot] = vout
        return vout

    def emit_xpose16(slot, vln, dst_fn, extra0=()):
        evs = []
        for c in range(16):
            pp = ps_a if c % 2 == 0 else ps_b
            w = [("V", vln)] if c == 0 else []
            if c < 2:
                w += list(extra0) + v_consts
            if c >= 2:
                w.append(("S", evs[c - 2]))
            vt = T(lambda t_, c=c, slot=slot, pp=pp:
                   t_.transpose(pp.bitcast(BF16)[:, 0:128],
                                xn_rows[:, slot, c * 128:(c + 1) * 128], identb_sb[:, :]),
                   waits=w)
            evs.append(S(lambda s, c=c, pp=pp, dst=dst_fn(c):
                         s.copy(out=dst, in_=pp.bitcast(BF16)[:, 0:128]),
                         waits=[("T", vt)],
                         rd=[pp.bitcast(BF16)[:, 0:128]], wr=[dst_fn(c)]))
        return evs

    for t in range(8 if stage >= 0.45 else 1):
        slot = t % 2
        vln = emit_ln(x_own[t * 128:(t + 1) * 128, :], slot)
        vdump = p.dma("scalar", ag_xn_in[t * 128:(t + 1) * 128, :], xn_rows[:, slot, :],
                      waits=[("V", vln)])
        evs = emit_xpose16(slot, vln, lambda c, t=t: xnT[:, c, t * 128:(t + 1) * 128],
                           extra0=([("V", v_bias)] if t == 0 else ()))
        xnr_guard[slot] = [vdump, ("S", evs[-1])]
    v_xnT = p.val("S")

    if stage < 0.7:
        _finish(p, nc)
        return nc, dbg
    v_agxn = p.qbar("scalar")
    CCe(lambda g: g.collective_compute("AllGather", ALU.bypass, RG, [ag_xn_in], [xn_all]),
        waits=v_agxn)
    v_cc_xn = p.val("CC")

    for i in range(65):
        p.dma("gpsimd", partial.rearrange("(a p) d -> a p d", p=128)[i], zero_v,
              waits=[("G", vz)] if i == 0 else ())
    v_pzero = p.qbar("gpsimd")

    if stage < 0.8:
        _finish(p, nc)
        return nc, dbg
    v_sigp = 0
    for t in range(8):
        for c in range(16):
            w = ([("S", v_xnT), ("V", v_bias)] + v_pc_gw) if (t == 0 and c == 0) else []
            if c == 0 and t > 0:
                w = [("S", v_sigp)]
            T(lambda t_, c=c, t=t: t_.matmul(ps_b[0:32, 0:128], gwT_sb[:, c, :],
                                             xnT[:, c, t * 128:(t + 1) * 128],
                                             start=(c == 0), stop=(c == 15)), waits=w)
        vq = p.val("T")
        v_sigp = S(lambda s, t=t: s.activation(sigT[:, t * 128:(t + 1) * 128], ps_b[0:32, 0:128],
                                      ACT.Sigmoid), waits=[("T", vq)],
                   rd=[ps_b[0:32, 0:128]], wr=[sigT[:, t * 128:(t + 1) * 128]])
    v_sig = p.val("S")
    for t in range(8):
        w = ([("S", v_sig)] + v_consts) if t == 0 else [("V", p.val("V"))]
        vt = T(lambda t_, t=t: t_.transpose(ps_a[:, 0:32],
                                            sigT[0:32, t * 128:(t + 1) * 128],
                                            identf_sb[0:32, 0:32]), waits=w)
        V(lambda v, t=t: v.tensor_copy(sc_rows[:, t, :], ps_a[:, 0:32]), waits=[("T", vt)],
          rd=[ps_a[:, 0:32]], wr=[sc_rows[:, t, :]])
    v_scr = p.val("V")
    v_scd = p.dma("scalar", ag_sc_in.rearrange("(t p) e -> p t e", p=128),
                  sc_rows[:, :, :], waits=[("V", v_scr)])
    CCe(lambda g: g.collective_compute("AllGather", ALU.bypass, RG, [ag_sc_in],
                                       [scores_all]), waits=[v_scd])
    v_cc_sc = p.val("CC")

    if debug:
        p.dma("sync", dbg["xn_all_o"][:, :], xn_all[:, :], waits=[("CC", v_cc_xn)])
        p.dma("sync", dbg["scores_o"][:, :], scores_all[:, :], waits=[("CC", v_cc_sc)])

    if stage < 2:
        _finish(p, nc)
        return nc, dbg

    st_ = {"ktr": 0, "kv_ev": [0, 0], "q_ev": [0, 0], "s_ld": {}, "tp_ev": [0, 0]}

    def attn_pass(PS):
        exp_l, ve2_l, enc_l = [], [], []
        op_guard = []
        stamp = p.val("V")
        st_["kv_ev"] = [stamp, stamp]
        st_["q_ev"] = [stamp, stamp]
        ln_guard[0] = max(ln_guard[0], stamp)
        ln_guard[1] = max(ln_guard[1], stamp)
        for half in range(2):
            vw_kv = p.dma("sync", wst[:, :, :],
                          wkv_bf.rearrange("(c p) n -> p c n", p=128)
                          [:, :, PS * 1024 + half * 512:PS * 1024 + (half + 1) * 512],
                          waits=bar("T", "V") + v_pc_kv)
            for ti in range(16):
                v_ring = 0
                if ti < 8:
                    slot = ti % 2
                    vln = emit_ln(x_halo[ti * 128:(ti + 1) * 128, :], slot)
                    rc = (ti % 4) * 128
                    evs = emit_xpose16(slot, vln,
                                       lambda c, rc=rc: halo_ring[:, c, rc:rc + 128],
                                       extra0=bar("S"))
                    xnr_guard[slot] = [("S", evs[-1])]
                    v_ring = evs[-1]
                pb = ps_l1 if ti % 2 == 0 else ps_l2
                for c in range(16):
                    w = []
                    if c == 0:
                        w = [vw_kv, ("V", st_["kv_ev"][ti % 2])]
                        w.append(("S", v_ring if ti < 8 else v_xnT))
                    src = (halo_ring[:, c, (ti % 4) * 128:(ti % 4) * 128 + 128] if ti < 8
                           else xnT[:, c, (ti - 8) * 128:(ti - 8) * 128 + 128])
                    T(lambda t_, c=c, src=src, pb=pb:
                      t_.matmul(pb[:, 0:512], src, wst[:, c, :],
                                start=(c == 0), stop=(c == 15)), waits=w)
                vkvm = p.val("T")
                for i2 in range(2):
                    kvh = half * 2 + i2
                    V(lambda v, i2=i2, pb=pb:
                      v.tensor_copy(rows_a[:, i2 * 128:(i2 + 1) * 128],
                                    pb[:, i2 * 256:i2 * 256 + 128]),
                      waits=[("T", vkvm), ("T", st_["ktr"])] if i2 == 0 else (),
                      rd=[pb[:, i2 * 256:i2 * 256 + 128]],
                      wr=[rows_a[:, i2 * 128:(i2 + 1) * 128]])
                    V(lambda v, i2=i2, kvh=kvh, ti=ti, pb=pb:
                      v.tensor_copy(v_sb[:, ti, kvh, :],
                                    pb[:, i2 * 256 + 128:i2 * 256 + 256]),
                      rd=[pb[:, i2 * 256 + 128:i2 * 256 + 256]],
                      wr=[v_sb[:, ti, kvh, :]])
                nh = 2
                _rms_rope(nh, rope_k, ti, scale_k, False)
                vv = p.val("V")
                st_["kv_ev"][ti % 2] = vv
                for i2 in range(2):
                    kvh = half * 2 + i2
                    pp = ps_a if i2 % 2 == 0 else ps_b
                    w = ([("V", vv)] + v_consts) if i2 == 0 else []
                    w.append(("V", st_["tp_ev"][i2 % 2]))
                    vt = T(lambda t_, i2=i2, pp=pp:
                           t_.transpose(pp[:, 0:128], rows_b[:, i2 * 128:(i2 + 1) * 128],
                                        identf_sb[:, :]), waits=w)
                    st_["ktr"] = vt
                    st_["tp_ev"][i2 % 2] = V(lambda v, kvh=kvh, ti=ti, pp=pp:
                      v.tensor_copy(kT[:, kvh, ti * 128:(ti + 1) * 128], pp[:, 0:128]),
                      waits=[("T", vt)],
                      rd=[pp[:, 0:128]], wr=[kT[:, kvh, ti * 128:(ti + 1) * 128]])
        for half in range(2):
            vw_q = p.dma("sync", wst[:, :, :],
                         wq_bf.rearrange("(c p) n -> p c n", p=128)
                         [:, :, PS * 1024 + half * 512:PS * 1024 + (half + 1) * 512],
                         waits=bar("T", "V") + v_pc_q)
            for ti in range(8):
                pb = ps_l1 if ti % 2 == 0 else ps_l2
                for c in range(16):
                    w = [vw_q, ("V", st_["q_ev"][ti % 2])] if c == 0 else []
                    T(lambda t_, c=c, ti=ti, pb=pb:
                      t_.matmul(pb[:, 0:512], xnT[:, c, ti * 128:(ti + 1) * 128],
                                wst[:, c, :], start=(c == 0), stop=(c == 15)), waits=w)
                vqm = p.val("T")
                V(lambda v, pb=pb: v.tensor_copy(rows_a[:, 0:512], pb[:, 0:512]),
                  waits=[("T", vqm), ("T", st_["ktr"])],
                  rd=[pb[:, 0:512]], wr=[rows_a[:, 0:512]])
                _rms_rope(4, rope_q, ti, scale_q, True)
                vv = p.val("V")
                st_["q_ev"][ti % 2] = vv
                for i2 in range(4):
                    qh = half * 4 + i2
                    pp = ps_a if i2 % 2 == 0 else ps_b
                    w = ([("V", vv)] + v_consts) if i2 == 0 else []
                    w.append(("V", st_["tp_ev"][i2 % 2]))
                    vt = T(lambda t_, i2=i2, pp=pp:
                           t_.transpose(pp[:, 0:128], rows_b[:, i2 * 128:(i2 + 1) * 128],
                                        identf_sb[:, :]), waits=w)
                    st_["ktr"] = vt
                    st_["tp_ev"][i2 % 2] = V(lambda v, qh=qh, ti=ti, pp=pp:
                      v.tensor_copy(qT[:, qh, ti * 128:(ti + 1) * 128], pp[:, 0:128]),
                      waits=[("T", vt)],
                      rd=[pp[:, 0:128]], wr=[qT[:, qh, ti * 128:(ti + 1) * 128]])
        mask_ev = {}
        ve1_l = []
        stA = {}

        def emit_AT(gi):
            tau, it = divmod(gi, 8)
            kvh, gg = divmod(it, 2)
            qh = kvh * 2 + gg
            par = gi % 2
            pb = ps_l1 if par == 0 else ps_l2
            if it == 0:
                wm = bar("V", "S")
                mask_ev[tau] = p.dma("sync", maskt[:, :], maskin[tau], waits=wm)
            v_mask = mask_ev[tau]
            w0 = [v_mask]
            if gi >= 2:
                w0.append(("S", exp_l[gi - 2]))
            for ci, (c0, cn) in enumerate([(0, 512), (512, 512), (1024, 128)]):
                T(lambda t_, tau=tau, kvh=kvh, qh=qh, c0=c0, cn=cn, pb=pb:
                  t_.matmul(pb[:, c0:c0 + cn], qT[:, qh, tau * 128:(tau + 1) * 128],
                            kT[:, kvh, tau * 128 + c0:tau * 128 + c0 + cn],
                            start=True, stop=True), waits=w0 if ci == 0 else ())
                w0 = []
            vlg = p.val("T")
            vmx = V(lambda v, par=par, pb=pb:
                    v.tensor_reduce(negmax[:, par:par + 1], pb[:, 0:1152], AXX,
                                    ALU.max, negate=True), waits=[("T", vlg)],
                    rd=[pb[:, 0:1152]], wr=[negmax[:, par:par + 1]])
            stA[gi] = [tau, it, kvh, qh, par, pb, v_mask, vmx, None]

        def emit_AS(gi):
            tau, it, kvh, qh, par, pb, v_mask, vmx, _ = stA[gi]
            vex = S(lambda s, par=par, pb=pb:
                    s.activation(probs[:, par, :], pb[:, 0:1152], ACT.Exp,
                                 bias=negmax[:, par:par + 1]), waits=[("V", vmx)],
                    rd=[pb[:, 0:1152], negmax[:, par:par + 1]], wr=[probs[:, par, :]])
            exp_l.append(vex)
            V(lambda v, par=par:
              v.scalar_tensor_tensor(probs[:, par, :], probs[:, par, :], 1.0,
                                     maskt[:, :], ALU.mult, ALU.mult,
                                     accum_out=dsum[:, par:par + 1]),
              waits=[("S", vex), v_mask],
              rd=[probs[:, par, :], maskt[:, :]],
              wr=[probs[:, par, :], dsum[:, par:par + 1]])
            V(lambda v, par=par: v.reciprocal(dsum[:, par:par + 1],
                                              dsum[:, par:par + 1]),
              rd=[dsum[:, par:par + 1]], wr=[dsum[:, par:par + 1]])
            vnrm = V(lambda v, par=par:
                     v.tensor_scalar(probs[:, par, :], probs[:, par, :],
                                     dsum[:, par:par + 1], None, ALU.mult),
                     rd=[probs[:, par, :], dsum[:, par:par + 1]],
                     wr=[probs[:, par, :]])
            stA[gi][8] = vnrm

        def emit_B1a(gi):
            nonlocal op_guard
            tau, it, kvh, qh, par, pb, v_mask, vmx, vnrm = stA[gi]
            for c in range(8):
                w = []
                if c == 0:
                    w = [("V", vnrm)] + op_guard
                    if ve1_l:
                        w.append(("S", ve1_l[-1]))
                    op_guard = []
                T(lambda t_, c=c, par=par:
                  t_.transpose(ps_a.bitcast(BF16)[:, c * 128:(c + 1) * 128],
                               probs[:, par, c * 128:(c + 1) * 128], identb_sb[:, :]),
                  waits=w)
            vt9 = T(lambda t_, par=par, pb=pb:
                    t_.transpose(pb.bitcast(BF16)[:, 2304:2432],
                                 probs[:, par, 1024:1152], identb_sb[:, :]))
            stA[gi].append(vt9)

        def emit_B1b(gi):
            vt9 = stA[gi][9]
            par = stA[gi][4]
            pb = stA[gi][5]
            ve1 = S(lambda s: s.copy(out=probsT[:, 0:8, :].rearrange("p a b -> p (a b)"),
                                     in_=ps_a.bitcast(BF16)[:, 0:1024]),
                    waits=[("T", vt9)],
                    rd=[ps_a.bitcast(BF16)[:, 0:1024]],
                    wr=[probsT[:, 0:8, :]])
            ve1_l.append(ve1)
            ve2 = S(lambda s, pb=pb: s.copy(out=probsT[:, 8, :],
                                            in_=pb.bitcast(BF16)[:, 2304:2432]),
                    waits=[("T", vt9)],
                    rd=[pb.bitcast(BF16)[:, 2304:2432]], wr=[probsT[:, 8, :]])
            ve2_l.append(ve2)

        def emit_B2(gi):
            tau, it, kvh, qh, par, pb = stA.pop(gi)[0:6]
            ve1 = ve1_l[gi]
            ve2 = ve2_l[gi]
            for c in range(WIN):
                w = []
                if c == 0:
                    w = [("S", ve2)]
                    if enc_l:
                        w.append(("V", enc_l[-1]))
                vav = T(lambda t_, c=c, tau=tau, kvh=kvh:
                        t_.matmul(ps_b[:, 0:128],
                                  v_sb[:, tau + c, kvh, :], probsT[:, c, :],
                                  start=(c == 0), stop=(c == WIN - 1)), waits=w)
            enc_l.append(V(lambda v, qh=qh, tau=tau:
                           v.tensor_copy(encT[:, qh, tau % 2, :],
                                         ps_b[:, 0:128]),
                           waits=[("T", vav)],
                           rd=[ps_b[:, 0:128]], wr=[encT[:, qh, tau % 2, :]]))
            if it == 7 and tau % 2 == 1:
                emit_outproj(tau)

        def emit_outproj(tau):
            nonlocal op_guard
            if True:
                v_enc = enc_l[-1]
                opmm = []
                opev = []
                sld = {}
                if PS == 1:
                    for tpar in range(2):
                        tt = tau - 1 + tpar
                        sld[tpar] = p.dma("sync", s_stage[:, tpar, :],
                                          s_dram[tt * 128:(tt + 1) * 128, :],
                                          waits=[st_["s_ld"][tt % 2]] if (tt % 2) in st_["s_ld"] else ())
                for dch in range(4):
                    for nh in range(8):
                        n_global = PS * 8 + nh
                        wbuf2 = nh % 2
                        wv = [("T", opmm[-3])] if len(opmm) >= 3 else (bar("T") + v_pc_wo)
                        vwo = p.dma("sync", wo_sb[:, wbuf2, :],
                                    wo_bf[n_global][:, dch * 512:(dch + 1) * 512],
                                    waits=wv)
                        for tpar in range(2):
                            pbo = ps_a if tpar == 0 else ps_b
                            w = [vwo]
                            if nh == 0:
                                w.append(("V", v_enc))
                                w.append(("S", ve2_l[-1]))
                                if opev:
                                    w.append(opev[-2 + tpar] if len(opev) >= 2 else opev[-1])
                            vo = T(lambda t_, nh=nh, tpar=tpar, pbo=pbo, wbuf2=wbuf2:
                                   t_.matmul(pbo[:, 0:512], encT[:, nh, tpar, :],
                                             wo_sb[:, wbuf2, :], start=(nh == 0),
                                             stop=(nh == 7)), waits=w)
                            opmm.append(vo)
                            if nh == 7:
                                if PS == 0:
                                    ve = ("S", S(lambda s, tpar=tpar, dch=dch, pbo=pbo:
                                                 s.copy(out=s_stage[:, tpar, dch * 512:(dch + 1) * 512],
                                                        in_=pbo[:, 0:512]),
                                                 waits=[("T", vo)],
                                                 rd=[pbo[:, 0:512]],
                                                 wr=[s_stage[:, tpar, dch * 512:(dch + 1) * 512]]))
                                else:
                                    ve = ("V", V(lambda v, tpar=tpar, dch=dch, pbo=pbo:
                                                 v.tensor_tensor(s_stage[:, tpar, dch * 512:(dch + 1) * 512],
                                                                 pbo[:, 0:512],
                                                                 s_stage[:, tpar, dch * 512:(dch + 1) * 512],
                                                                 ALU.add),
                                                 waits=[("T", vo), sld[tpar]],
                                                 rd=[pbo[:, 0:512],
                                                     s_stage[:, tpar, dch * 512:(dch + 1) * 512]],
                                                 wr=[s_stage[:, tpar, dch * 512:(dch + 1) * 512]]))
                                opev.append(ve)
                for tpar in range(2):
                    tt = tau - 1 + tpar
                    vsd = p.dma("scalar", s_dram[tt * 128:(tt + 1) * 128, :],
                                s_stage[:, tpar, :], waits=[opev[6 + tpar]])
                    st_["s_ld"][tt % 2] = vsd
                op_guard = [opev[-1], opev[-2]]

        emit_AT(0)
        emit_AS(0)
        emit_AT(1)
        for gi in range(64):
            emit_B1a(gi)
            if gi + 1 < 64:
                emit_AS(gi + 1)
            emit_B1b(gi)
            if gi + 2 < 64:
                emit_AT(gi + 2)
            emit_B2(gi)
        return p.qbar("scalar")

    def _rms_rope(nh, rope, ti, scale, is_q, ab=0):
        n = nh * 128
        V(lambda v: v.tensor_tensor(rms_t[:, 0:n], rows_a[:, 0:n], rows_a[:, 0:n],
                                    ALU.mult), waits=v_consts,
          rd=[rows_a[:, 0:n]], wr=[rms_t[:, 0:n]])
        vv = V(lambda v: v.tensor_reduce(rms_s[:, 0:nh],
                                         rms_t.rearrange("p (a h) -> p a h", h=128)[:, 0:nh, :],
                                         AXX, ALU.add),
               rd=[rms_t[:, 0:n]], wr=[rms_s[:, 0:nh]])
        S(lambda s: s.activation(rms_s[:, 8:8 + nh], rms_s[:, 0:nh], ACT.Ln,
                                 scale=actc_sb[:, 1:2], bias=actc_sb[:, 0:1]),
          waits=[("V", vv)] + v_consts,
          rd=[rms_s[:, 0:nh], actc_sb[:, 0:2]], wr=[rms_s[:, 8:8 + nh]])
        if is_q:
            vs = S(lambda s: s.activation(rms_s[:, 8:8 + nh], rms_s[:, 8:8 + nh],
                                          ACT.Exp, scale=actc_sb[:, 2:3], bias=actc_sb[:, 3:4]),
                   rd=[rms_s[:, 8:8 + nh], actc_sb[:, 2:4]], wr=[rms_s[:, 8:8 + nh]])
        else:
            vs = S(lambda s: s.activation(rms_s[:, 8:8 + nh], rms_s[:, 8:8 + nh],
                                          ACT.Exp, scale=actc_sb[:, 2:3]),
                   rd=[rms_s[:, 8:8 + nh], actc_sb[:, 2:3]], wr=[rms_s[:, 8:8 + nh]])
        ra = rows_a.rearrange("p (a h) -> p a h", h=128)
        rb = rows_b.rearrange("p (a h) -> p a h", h=128)
        rt = rms_t.rearrange("p (a h) -> p a h", h=128)
        V(lambda v: v.tensor_tensor(ra[:, 0:nh, :], ra[:, 0:nh, :],
                                    rms_s[:, 8:8 + nh].unsqueeze(2).broadcast_to([128, nh, 128]),
                                    ALU.mult), waits=[("S", vs)],
          rd=[rows_a[:, 0:n], rms_s[:, 8:8 + nh]], wr=[rows_a[:, 0:n]])
        V(lambda v: v.tensor_tensor(ra[:, 0:nh, :], ra[:, 0:nh, :],
                                    scale[:, :].unsqueeze(1).broadcast_to([128, nh, 128]),
                                    ALU.mult),
          rd=[rows_a[:, 0:n], scale[:, :]], wr=[rows_a[:, 0:n]])
        sin = rope[:, 0, ti, :].unsqueeze(1).broadcast_to([128, nh, 64])
        cos = rope[:, 1, ti, :].unsqueeze(1).broadcast_to([128, nh, 64])
        rdrope = [rope[:, 0, ti, :], rope[:, 1, ti, :]]
        rbs = rows_b[:, ab * 128:ab * 128 + n]
        V(lambda v: v.tensor_tensor(rb[:, ab:ab + nh, 0:64], ra[:, 0:nh, 0:64], cos, ALU.mult),
          rd=[rows_a[:, 0:n]] + rdrope, wr=[rbs])
        V(lambda v: v.tensor_tensor(rt[:, 0:nh, 0:64], ra[:, 0:nh, 64:128], sin, ALU.mult),
          rd=[rows_a[:, 0:n]] + rdrope, wr=[rms_t[:, 0:n]])
        V(lambda v: v.tensor_tensor(rb[:, ab:ab + nh, 0:64], rb[:, ab:ab + nh, 0:64],
                                    rt[:, 0:nh, 0:64], ALU.subtract),
          rd=[rbs, rms_t[:, 0:n]], wr=[rbs])
        V(lambda v: v.tensor_tensor(rb[:, ab:ab + nh, 64:128], ra[:, 0:nh, 64:128], cos, ALU.mult),
          rd=[rows_a[:, 0:n]] + rdrope, wr=[rbs])
        V(lambda v: v.tensor_tensor(rt[:, 0:nh, 0:64], ra[:, 0:nh, 0:64], sin, ALU.mult),
          rd=[rows_a[:, 0:n]] + rdrope, wr=[rms_t[:, 0:n]])
        V(lambda v: v.tensor_tensor(rb[:, ab:ab + nh, 64:128], rb[:, ab:ab + nh, 64:128],
                                    rt[:, 0:nh, 0:64], ALU.add),
          rd=[rbs, rms_t[:, 0:n]], wr=[rbs])

    attn_pass(0)

    vw = p.dma("sync", sc_all[:, :, :], scores_all.rearrange("(t p) e -> p t e", p=128),
               waits=[("CC", v_cc_sc)])
    SCA, SELT, SELR, POST = sc_all[:, :, :], sel_t[:, :, :], selr[:, :, :], pos_t[:, :, :]
    RA, RB, RC, RD = r_a[:, :, :], r_b[:, :, :], r_c[:, :, :], r_d[:, :, :]
    RK, SG = rank_t[:, :, :], selg[:, :, :]
    V(lambda v: v.tensor_tensor(sc_all[:, :, :], sc_all[:, :, :],
                                bias_t[:, :].unsqueeze(1).broadcast_to([128, 64, 32]), ALU.add),
      waits=[vw, ("V", v_bias)], rd=[SCA, bias_t[:, :]], wr=[SCA])
    g4 = sc_all.rearrange("p t (g f) -> p t g f", f=4)
    V(lambda v: v.tensor_tensor(r_a[:, :, :], g4[:, :, :, 0], g4[:, :, :, 1], ALU.max),
      rd=[SCA], wr=[RA])
    V(lambda v: v.tensor_tensor(r_b[:, :, :], g4[:, :, :, 0], g4[:, :, :, 1], ALU.min),
      rd=[SCA], wr=[RB])
    V(lambda v: v.tensor_tensor(r_c[:, :, :], g4[:, :, :, 2], g4[:, :, :, 3], ALU.max),
      rd=[SCA], wr=[RC])
    V(lambda v: v.tensor_tensor(r_d[:, :, :], g4[:, :, :, 2], g4[:, :, :, 3], ALU.min),
      rd=[SCA], wr=[RD])
    V(lambda v: v.tensor_tensor(selg[:, :, :], r_a[:, :, :], r_c[:, :, :], ALU.is_ge),
      rd=[RA, RC], wr=[SG])
    V(lambda v: v.tensor_tensor(rank_t[:, :, :], r_b[:, :, :], r_d[:, :, :], ALU.subtract),
      rd=[RB, RD], wr=[RK])
    V(lambda v: v.tensor_tensor(rank_t[:, :, :], rank_t[:, :, :], selg[:, :, :], ALU.mult),
      rd=[RK, SG], wr=[RK])
    V(lambda v: v.tensor_tensor(rank_t[:, :, :], rank_t[:, :, :], r_d[:, :, :], ALU.add),
      rd=[RK, RD], wr=[RK])
    V(lambda v: v.tensor_tensor(r_b[:, :, :], r_a[:, :, :], r_c[:, :, :], ALU.min),
      rd=[RA, RC], wr=[RB])
    V(lambda v: v.tensor_tensor(rank_t[:, :, :], rank_t[:, :, :], r_b[:, :, :], ALU.max),
      rd=[RK, RB], wr=[RK])
    V(lambda v: v.tensor_tensor(r_a[:, :, :], r_a[:, :, :], r_c[:, :, :], ALU.max),
      rd=[RA, RC], wr=[RA])
    V(lambda v: v.tensor_tensor(r_a[:, :, :], r_a[:, :, :], rank_t[:, :, :], ALU.add),
      rd=[RA, RK], wr=[RA])
    V(lambda v: v.memset(rank_t[:, :, :], 0.0), rd=[], wr=[RK])
    for dd in range(1, 8):
        n = 8 - dd
        V(lambda v, dd=dd, n=n: v.tensor_tensor(r_b[:, :, 0:n], r_a[:, :, dd:8],
                                                r_a[:, :, 0:n], ALU.is_gt),
          rd=[RA], wr=[RB])
        V(lambda v, dd=dd, n=n: v.tensor_tensor(rank_t[:, :, 0:n], rank_t[:, :, 0:n],
                                                r_b[:, :, 0:n], ALU.add),
          rd=[RK, RB], wr=[RK])
        V(lambda v, dd=dd, n=n: v.tensor_tensor(r_b[:, :, dd:8], r_a[:, :, 0:n],
                                                r_a[:, :, dd:8], ALU.is_ge),
          rd=[RA], wr=[RB])
        V(lambda v, dd=dd, n=n: v.tensor_tensor(rank_t[:, :, dd:8], rank_t[:, :, dd:8],
                                                r_b[:, :, dd:8], ALU.add),
          rd=[RK, RB], wr=[RK])
    V(lambda v: v.tensor_scalar(selg[:, :, :], rank_t[:, :, :], 3.5, None, ALU.is_lt),
      rd=[RK], wr=[SG])
    V(lambda v: v.tensor_tensor(sc_all.rearrange("p t (g f) -> p t g f", f=4), g4,
                                selg.unsqueeze(3).broadcast_to([128, 64, 8, 4]), ALU.mult),
      rd=[SCA, SG], wr=[SCA])
    V(lambda v: v.tensor_scalar(selr[:, :, :], sc_all[:, :, :], 0.0, None, ALU.is_equal),
      rd=[SCA], wr=[SELR])
    V(lambda v: v.tensor_tensor(selr[:, :, :], selr[:, :, :],
                                epse_t[:, :].unsqueeze(1).broadcast_to([128, 64, 32]), ALU.mult),
      rd=[SELR, epse_t[:, :]], wr=[SELR])
    V(lambda v: v.tensor_tensor(sc_all[:, :, :], sc_all[:, :, :], selr[:, :, :],
                                ALU.subtract), rd=[SCA, SELR], wr=[SCA])
    V(lambda v: v.memset(sel_t[:, :, :], 0.0), rd=[], wr=[SELT])
    for r in range(4):
        V(lambda v: v.tensor_reduce(m_max[:, :], sc_all[:, :, :], AXX, ALU.max),
          rd=[SCA], wr=[m_max[:, :]])
        V(lambda v: v.tensor_tensor(selr[:, :, :], sc_all[:, :, :],
                                    m_max[:, :].unsqueeze(2).broadcast_to([128, 64, 32]),
                                    ALU.is_ge),
          rd=[SCA, m_max[:, :]], wr=[SELR])
        V(lambda v: v.tensor_tensor(sel_t[:, :, :], sel_t[:, :, :], selr[:, :, :], ALU.add),
          rd=[SELT, SELR], wr=[SELT])
        V(lambda v: v.scalar_tensor_tensor(sc_all[:, :, :], selr[:, :, :], -1e30,
                                           sc_all[:, :, :], ALU.mult, ALU.add),
          rd=[SCA, SELR], wr=[SCA])
    vw2 = p.dma("sync", sc_all[:, :, :], scores_all.rearrange("(t p) e -> p t e", p=128),
                waits=bar("V"))
    V(lambda v: v.tensor_tensor(sc_all[:, :, :], sc_all[:, :, :], sel_t[:, :, :],
                                ALU.mult), waits=[vw2], rd=[SCA, SELT], wr=[SCA])
    V(lambda v: v.tensor_reduce(wsum[:, :], sc_all[:, :, :], AXX, ALU.add),
      rd=[SCA], wr=[wsum[:, :]])
    V(lambda v: v.tensor_scalar(wsum[:, :], wsum[:, :], 1e-20, None, ALU.add),
      rd=[wsum[:, :]], wr=[wsum[:, :]])
    V(lambda v: v.reciprocal(wsum[:, :], wsum[:, :]),
      rd=[wsum[:, :]], wr=[wsum[:, :]])
    V(lambda v: v.tensor_tensor(sc_all[:, :, :], sc_all[:, :, :],
                                wsum[:, :].unsqueeze(2).broadcast_to([128, 64, 32]), ALU.mult),
      rd=[SCA, wsum[:, :]], wr=[SCA])
    V(lambda v: v.tensor_scalar(sc_all[:, :, :], sc_all[:, :, :], 2.5, None, ALU.mult),
      rd=[SCA], wr=[SCA])
    v_selbf = V(lambda v: v.tensor_copy(sel_bf[:, :, :], sel_t[:, :, :]),
                rd=[SELT], wr=[sel_bf[:, :, :]])
    cnt_ev = 0
    for j in range(4):
        w = bar("V", "S") + [("V", v_selbf)] if j == 0 else [("V", cnt_ev)]
        vt = T(lambda t_, j=j: t_.matmul(ps_a[0:1, 0:512], ones_cb_sb[:, :],
                                         sel_bf.rearrange("p t e -> p (t e)")[:, j * 512:(j + 1) * 512],
                                         start=True, stop=True), waits=w)
        cnt_ev = V(lambda v, j=j: v.tensor_copy(cnt_t.rearrange("p t e -> p (t e)")[0:1, j * 512:(j + 1) * 512],
                                                ps_a[0:1, 0:512]), waits=[("T", vt)],
                   rd=[ps_a[0:1, 0:512]],
                   wr=[cnt_t.rearrange("p t e -> p (t e)")[0:1, j * 512:(j + 1) * 512]])
    V(lambda v: v.memset(off_t[0:1, 0, :], 0.0), waits=[("V", cnt_ev)],
      rd=[], wr=[off_t[0:1, 0, :]])
    for t in range(1, 64):
        V(lambda v, t=t: v.tensor_tensor(off_t[0:1, t, :], off_t[0:1, t - 1, :],
                                         cnt_t[0:1, t - 1, :], ALU.add),
          rd=[off_t[0:1, t - 1, :], cnt_t[0:1, t - 1, :]], wr=[off_t[0:1, t, :]])
    v_off = p.val("V")
    pos_ev = []
    for t in range(64):
        pbp = ps_b if t % 2 == 0 else ps_a
        w = ([("V", v_off)] + v_consts) if t == 0 else []
        if t >= 2:
            w.append(("V", pos_ev[t - 2]))
        T(lambda t_, t=t, pbp=pbp: t_.matmul(pbp[:, 0:32], tri_cum_sb[:, :],
                                             sel_bf[:, t, :], start=True, stop=False),
          waits=w)
        vp = T(lambda t_, t=t, pbp=pbp: t_.matmul(pbp[:, 0:32], ones_f_sb[:1, :],
                                                  off_t[0:1, t, :], start=False, stop=True))
        pos_ev.append(V(lambda v, t=t, pbp=pbp: v.tensor_copy(pos_t[:, t, :],
                                                              pbp[:, 0:32]),
                        waits=[("T", vp)],
                        rd=[pbp[:, 0:32]], wr=[pos_t[:, t, :]]))
    V(lambda v: v.tensor_scalar(selr[:, :, :], pos_t[:, :, :], float(CAP), None, ALU.is_lt),
      rd=[POST], wr=[SELR])
    V(lambda v: v.tensor_tensor(selr[:, :, :], selr[:, :, :], sel_t[:, :, :], ALU.mult),
      rd=[SELR, SELT], wr=[SELR])
    V(lambda v: v.tensor_scalar(pos_t[:, :, :], pos_t[:, :, :], 1.0, None, ALU.add),
      rd=[POST], wr=[POST])
    V(lambda v: v.tensor_tensor(pos_t[:, :, :], pos_t[:, :, :], selr[:, :, :], ALU.mult),
      rd=[POST, SELR], wr=[POST])
    V(lambda v: v.tensor_scalar(pos_t[:, :, :], pos_t[:, :, :], 1.0, None, ALU.subtract),
      rd=[POST], wr=[POST])
    V(lambda v: v.tensor_copy(sel_bf[:, :, :], sc_all[:, :, :]),
      rd=[SCA], wr=[sel_bf[:, :, :]])
    V(lambda v: v.tensor_tensor(sel_t[:, :, :], sc_all[:, :, :], sel_bf[:, :, :],
                                ALU.subtract),
      rd=[SCA, sel_bf[:, :, :]], wr=[SELT])
    for e in range(4):
        eb = esel_t[:, e, :].unsqueeze(1).broadcast_to([128, 64, 32])
        ebr = esel_t[:, e, :]
        V(lambda v, eb=eb: v.tensor_tensor(selr[:, :, :], pos_t[:, :, :], eb, ALU.mult),
          rd=[POST, ebr], wr=[SELR])
        V(lambda v, e=e: v.tensor_reduce(own3[:, e, :], selr[:, :, :], AXX, ALU.add),
          rd=[SELR], wr=[own3[:, e, :]])
        V(lambda v, eb=eb: v.tensor_tensor(selr[:, :, :], sel_bf[:, :, :], eb, ALU.mult),
          rd=[sel_bf[:, :, :], ebr], wr=[SELR])
        V(lambda v, e=e: v.tensor_reduce(own3[:, 4 + e, :], selr[:, :, :], AXX, ALU.add),
          rd=[SELR], wr=[own3[:, 4 + e, :]])
        V(lambda v, eb=eb: v.tensor_tensor(selr[:, :, :], sel_t[:, :, :], eb, ALU.mult),
          rd=[SELT, ebr], wr=[SELR])
        V(lambda v, e=e: v.tensor_reduce(own3[:, 8 + e, :], selr[:, :, :], AXX, ALU.add),
          rd=[SELR], wr=[own3[:, 8 + e, :]])
    V(lambda v: v.tensor_copy(own_i[:, :, :], own3[:, 0:4, :]),
      rd=[own3[:, 0:4, :]], wr=[own_i[:, :, :]])
    V(lambda v: v.tensor_copy(own_hi[:, :, :], own3[:, 4:8, :]),
      rd=[own3[:, 4:8, :]], wr=[own_hi[:, :, :]])
    v_own = V(lambda v: v.tensor_copy(own_lo[:, :, :], own3[:, 8:12, :]),
              rd=[own3[:, 8:12, :]], wr=[own_lo[:, :, :]])
    d1 = p.dma("scalar", st_dump[:, :, :], own_i[:, :, :], waits=[("V", v_own)])
    d2 = p.dma("scalar", whi_dump[:, :, :], own_hi[:, :, :])
    d3 = p.dma("scalar", wlo_dump[:, :, :], own_lo[:, :, :])
    for ee in range(4):
        rb1 = p.dma("sync", ls_idx[ee * 32:(ee + 1) * 32, :].rearrange("g (pm t) -> g pm t", pm=4),
                    st_dump.rearrange("(g pm) e t -> e g pm t", pm=4)[ee],
                    waits=[d1] if ee == 0 else ())
        rb2 = p.dma("sync", ls_whi[ee * 32:(ee + 1) * 32, :].rearrange("g (pm t) -> g pm t", pm=4),
                    whi_dump.rearrange("(g pm) e t -> e g pm t", pm=4)[ee],
                    waits=[d2] if ee == 0 else ())
        rb3 = p.dma("sync", ls_wlo[ee * 32:(ee + 1) * 32, :].rearrange("g (pm t) -> g pm t", pm=4),
                    wlo_dump.rearrange("(g pm) e t -> e g pm t", pm=4)[ee],
                    waits=[d3] if ee == 0 else ())
    for gq in range(4):
        G(lambda g, gq=gq: g.iota(ls_tok[gq * 32:(gq + 1) * 32, :],
                                  pattern=[[1, 4], [128, 64]],
                                  base=1, channel_multiplier=4),
          rd=[], wr=[ls_tok[gq * 32:(gq + 1) * 32, :]])
    v_iota = p.val("G")
    vls1 = G(lambda g: g.local_scatter(lso_tok[:, :], ls_tok[:, :], ls_idx[:, :],
                                       128, CAP, 256), waits=[rb1, ("G", v_iota)],
             rd=[ls_tok[:, :], ls_idx[:, :]], wr=[lso_tok[:, :]])
    vls2 = G(lambda g: g.local_scatter(lso_whi[:, :], ls_whi[:, :], ls_idx[:, :],
                                       128, CAP, 256), waits=[rb2],
             rd=[ls_whi[:, :], ls_idx[:, :]], wr=[lso_whi[:, :]])
    vls3 = G(lambda g: g.local_scatter(lso_wlo[:, :], ls_wlo[:, :], ls_idx[:, :],
                                       128, CAP, 256), waits=[rb3],
             rd=[ls_wlo[:, :], ls_idx[:, :]], wr=[lso_wlo[:, :]])
    vcf = V(lambda v: v.tensor_copy(lso_tok_f[:, :], lso_tok[:, :]), waits=[("G", vls1)],
            rd=[lso_tok[:, :]], wr=[lso_tok_f[:, :]])
    mg_ev = []
    for (src_sb, dst0, wg, add) in [(lso_tok_f, 0, [("V", vcf)], False),
                                    (lso_whi, CAP, [("G", vls2)], False),
                                    (lso_wlo, CAP, [("G", vls3)], True)]:
        lhs = i4f_sb if src_sb is lso_tok_f else i4b_sb
        for j in range(2):
            w = (wg + v_consts + (bar("V") if not mg_ev else [("V", mg_ev[-1])])) if j == 0 else \
                [("V", mg_ev[-1])]
            vt = T(lambda t_, j=j, src_sb=src_sb, lhs=lhs:
                   t_.matmul(ps_b[0:4, 0:512], lhs[:, :],
                             src_sb[:, j * 512:(j + 1) * 512], start=True, stop=True),
                   waits=w)
            if add:
                mg_ev.append(V(lambda v, j=j, dst0=dst0:
                               v.tensor_add(mrg[:, dst0 + j * 512:dst0 + (j + 1) * 512],
                                            mrg[:, dst0 + j * 512:dst0 + (j + 1) * 512],
                                            ps_b[0:4, 0:512]), waits=[("T", vt)],
                               rd=[mrg[:, dst0 + j * 512:dst0 + (j + 1) * 512],
                                   ps_b[0:4, 0:512]],
                               wr=[mrg[:, dst0 + j * 512:dst0 + (j + 1) * 512]]))
            else:
                mg_ev.append(V(lambda v, j=j, dst0=dst0:
                               v.tensor_copy(mrg[:, dst0 + j * 512:dst0 + (j + 1) * 512],
                                             ps_b[0:4, 0:512]), waits=[("T", vt)],
                               rd=[ps_b[0:4, 0:512]],
                               wr=[mrg[:, dst0 + j * 512:dst0 + (j + 1) * 512]]))
    V(lambda v: v.tensor_scalar(mrg[:, 2 * CAP:3 * CAP], mrg[:, 0:CAP], 0.0, None,
                                ALU.is_equal), waits=[("V", mg_ev[-1])],
      rd=[mrg[:, 0:CAP]], wr=[mrg[:, 2 * CAP:3 * CAP]])
    V(lambda v: v.tensor_scalar(mrg[:, 2 * CAP:3 * CAP], mrg[:, 2 * CAP:3 * CAP],
                                8193.0, None, ALU.mult),
      rd=[mrg[:, 2 * CAP:3 * CAP]], wr=[mrg[:, 2 * CAP:3 * CAP]])
    V(lambda v: v.tensor_scalar(mrg[:, 0:CAP], mrg[:, 0:CAP], 1.0, None, ALU.subtract),
      rd=[mrg[:, 0:CAP]], wr=[mrg[:, 0:CAP]])
    V(lambda v: v.tensor_scalar(gidx_i[:, :].bitcast(I16) if False else gidx_i[:, :],
                                mrg[:, 0:CAP], 0.0, None, ALU.max),
      rd=[mrg[:, 0:CAP]], wr=[gidx_i[:, :]])
    V(lambda v: v.tensor_add(mrg[:, 2 * CAP:3 * CAP], mrg[:, 2 * CAP:3 * CAP],
                             mrg[:, 0:CAP]),
      rd=[mrg[:, 2 * CAP:3 * CAP], mrg[:, 0:CAP]], wr=[mrg[:, 2 * CAP:3 * CAP]])
    v_sidx = V(lambda v: v.tensor_copy(sidx_i[:, :], mrg[:, 2 * CAP:3 * CAP]),
               rd=[mrg[:, 2 * CAP:3 * CAP]], wr=[sidx_i[:, :]])
    wt_ev = v_sidx
    for stt in range(8):
        vt = T(lambda t_, stt=stt:
               t_.transpose(ps_a[:, 0:4],
                            mrg[0:4, CAP + stt * 128:CAP + (stt + 1) * 128],
                            identf_sb[0:4, 0:4]),
               waits=[("V", wt_ev)] + (v_consts if stt == 0 else []))
        wt_ev = V(lambda v, stt=stt:
                  v.tensor_copy(wslot[:, :, stt], ps_a[:, 0:4]),
                  waits=[("T", vt)],
                  rd=[ps_a[:, 0:4]], wr=[wslot[:, :, stt]])
    v_wslot = wt_ev
    gidx_st = rbi[0:4, 0:1024]
    sidx_st = rbi[0:4, 1024:2048]
    V(lambda v: v.tensor_copy(gidx_st.rearrange("e (s j) -> e s j", s=16),
                              gidx_i.rearrange("e (j s) -> e s j", s=16)),
      waits=[("V", v_wslot)],
      rd=[gidx_i[:, :]], wr=[gidx_st])
    v_st2 = V(lambda v: v.tensor_copy(sidx_st.rearrange("e (s j) -> e s j", s=16),
                                      sidx_i.rearrange("e (j s) -> e s j", s=16)),
              rd=[sidx_i[:, :]], wr=[sidx_st])
    p.dma("scalar", gidx_dump[:, :], gidx_st, waits=[("V", v_st2)])
    p.dma("scalar", sidx_dump[:, :], sidx_st)
    v_idx_dumps = p.qbar("scalar")
    ln_dguard[0] = list(v_idx_dumps)

    if debug:
        p.dma("sync", dbg["gidx_o"][:, :], gidx_dump[:, :], waits=v_idx_dumps)
        p.dma("sync", dbg["sidx_o"][:, :], sidx_dump[:, :])
        p.dma("scalar", dbg["wslot_o"][:, :, :], wslot[:, :, :],
              waits=[("V", v_wslot)])

    if stage < 3:
        _finish(p, nc)
        return nc, dbg

    if stage >= 4:
        attn_pass(1)
    v_s_all = p.qbar("scalar")

    if debug:
        for tt in range(8):
            p.dma("sync", dbg["s_o"][tt * 128:(tt + 1) * 128, :],
                  s_dram[tt * 128:(tt + 1) * 128, :], waits=v_s_all)

    if stage < 5:
        _finish(p, nc)
        return nc, dbg

    wph = bar("V", "S", "T")
    v_scat = None
    v_gath = None
    prev_up_last = 0
    prev_dn_last = 0
    pending_scat = None
    pbg = {0: 0, 1: 0}
    for e in range(4):
        src_g = gidx_dump[e:e + 1, :].rearrange("o (s j) -> (o s) j", s=16)
        src_s = sidx_dump[e:e + 1, :].rearrange("o (s j) -> (o s) j", s=16)
        w_rb = list(v_idx_dumps)
        if e > 0:
            w_rb.append(v_gath)
        for gq in range(8):
            r2 = p.dma("sync", gidx_rep[16 * gq:16 * gq + 16, :], src_g,
                       waits=w_rb if gq == 0 else ())
        wg = [r2, ("CC", v_cc_xn)] + (wph if e == 0 else [])
        if e > 0:
            wg.append(("T", prev_up_last))
        for hfg in range(2):
            v_gath = Gd(lambda g, hfg=hfg: g.dma_gather(
                xeT2[hfg][:, :, :], xn_all,
                gidx_rep[:, hfg * 32:(hfg + 1) * 32], 512, 512, D,
                transpose=True, queue_num=1), waits=wg if hfg == 0 else ())
        up_src = up_wT.rearrange("e (c p) n -> e p c n", p=128)
        up_mm = []
        v_hT = 0
        for j in range(8):
            jb = j % 4
            w = (wph + v_idx_dumps) if (e == 0 and j == 0) else []
            if j >= 4:
                w = [("T", up_mm[2 * (j - 4) + 1])]
            elif e > 0 and j < 4:
                w = [("T", prev_up_last)]
            vwu = p.dma("gpsimd", wst[:, :, jb * 128:(jb + 1) * 128],
                        up_src[e][:, :, j * 128:(j + 1) * 128], waits=w)
            if j == 0:
                if pending_scat is not None:
                    wsc = [("V", pending_scat), r4] + v_pzero
                    if v_scat:
                        wsc.append(v_scat)
                    for hfg in range(2):
                        v_scat = Gd(lambda g, hfg=hfg, ep=e - 1: g.dma_scatter_add(
                            partial, yew[:, hfg * 4:(hfg + 1) * 4, :],
                            sidx_rep[:, hfg * 32:(hfg + 1) * 32], 512, 512, D,
                            queue_num=1), waits=wsc if hfg == 0 else ())
                    pending_scat = None
                for gq in range(8):
                    r4 = p.dma("sync", sidx_rep[16 * gq:16 * gq + 16, :], src_s,
                               waits=([v_scat] if v_scat else ()) if gq == 0 else ())
            for half in range(2):
                pbid = (j * 2 + half) % 2
                pb = ps_l1 if pbid == 0 else ps_l2
                for c in range(16):
                    w = [vwu, v_gath, ("V", pbg[pbid])] if c == 0 else []
                    vm = T(lambda t_, c=c, half=half, jb=jb, pb=pb:
                           t_.matmul(pb[:, 0:512], wst[:, c, jb * 128:(jb + 1) * 128],
                                     xeT2[half][:, c, :],
                                     start=(c == 0), stop=(c == 15)), waits=w)
                up_mm.append(vm)
                rlu = rows_a[:, pbid * 512:(pbid + 1) * 512]
                V(lambda v, pb=pb, rlu=rlu:
                  v.tensor_scalar(rlu, pb[:, 0:512], 0.0, None, ALU.max),
                  waits=[("T", vm)],
                  rd=[pb[:, 0:512]], wr=[rlu])
                v_hT = V(lambda v, j=j, half=half, pb=pb, rlu=rlu:
                         v.tensor_tensor(hT[:, j, half * 512:(half + 1) * 512],
                                         rlu, pb[:, 0:512], ALU.mult),
                         rd=[rlu, pb[:, 0:512]],
                         wr=[hT[:, j, half * 512:(half + 1) * 512]])
                pbg[pbid] = v_hT
        prev_up_last = up_mm[-1]
        dn_src = down_wT.rearrange("e (j p) n -> e p j n", p=128)
        yew_last = 0
        for dch in range(4):
            db = dch % 2
            w = [("V", v_hT)] if dch == 0 else [("T", prev_dn_last)]
            if e > 0 and dch == 0:
                w.append(("T", prev_dn_last))
            vwd = p.dma("gpsimd", dw_sb[:, db, :, :],
                        dn_src[e][:, :, dch * 512:(dch + 1) * 512], waits=w)
            for gst in range(8):
                pbid = gst % 2
                pb = ps_l1 if pbid == 0 else ps_l2
                for j in range(8):
                    w = [vwd, ("V", v_hT), ("V", pbg[pbid])] if j == 0 else []
                    vm = T(lambda t_, gst=gst, j=j, db=db, pb=pb:
                           t_.matmul(pb[:, 0:512], hT[:, j, gst * 128:(gst + 1) * 128],
                                     dw_sb[:, db, j, :], start=(j == 0), stop=(j == 7)),
                           waits=w)
                prev_dn_last = vm
                ve = V(lambda v, gst=gst, dch=dch, e=e, pb=pb:
                       v.tensor_scalar(yew[:, gst, dch * 512:(dch + 1) * 512],
                                       pb[:, 0:512], wslot[:, e, gst:gst + 1], None,
                                       ALU.mult), waits=[("T", vm)] + ([v_scat] if v_scat else []),
                       rd=[pb[:, 0:512], wslot[:, e, gst:gst + 1]],
                       wr=[yew[:, gst, dch * 512:(dch + 1) * 512]])
                pbg[pbid] = ve
                yew_last = ve
        pending_scat = yew_last

    wsc = [("V", pending_scat), r4] + v_pzero
    if v_scat:
        wsc.append(v_scat)
    for hfg in range(2):
        v_scat = Gd(lambda g, hfg=hfg: g.dma_scatter_add(
            partial, yew[:, hfg * 4:(hfg + 1) * 4, :],
            sidx_rep[:, hfg * 32:(hfg + 1) * 32], 512, 512, D,
            queue_num=1), waits=wsc if hfg == 0 else ())

    CCe(lambda g: g.collective_compute("ReduceScatter", ALU.add, RG,
                                       [partial[0:8192, :]], [routed]),
        waits=[v_scat])
    v_cc_rs = p.val("CC")

    if debug:
        p.dma("sync", dbg["routed_o"][:, :], routed[:, :], waits=[("CC", v_cc_rs)])

    if stage < 6:
        _finish(p, nc)
        return nc, dbg

    wph5 = bar("V", "S", "T")
    up_s = shup_bf.rearrange("(c p) n -> p c n", p=128)
    dn_s = shdn_bf.rearrange("(c p) n -> p c n", p=128)
    up5 = []
    pbg5 = {0: 0, 1: 0}
    cmb_last = 0
    out_tok = {0: None, 1: None}
    shdump = None
    for slab in range(4):
        h1_ev = 0
        pbg5[0] = max(pbg5[0], cmb_last)
        pbg5[1] = max(pbg5[1], cmb_last)
        for j in range(32):
            jb = j % 2
            w = (wph5 + v_pc_shup) if (slab == 0 and j == 0) else []
            if len(up5) >= 2:
                w = [("T", up5[-2])] + (v_pc_shup if (slab == 0 and j < 2) else [])
            vwu = p.dma("sync", shw_up[:, jb, :, :], up_s[:, :, j * 128:(j + 1) * 128],
                        waits=w)
            pbid = j % 2
            pb = ps_a if pbid == 0 else ps_b
            for c in range(16):
                w = [vwu, ("V", pbg5[pbid])] if c == 0 else []
                vm = T(lambda t_, j=j, c=c, jb=jb, slab=slab, pb=pb:
                       t_.matmul(pb[:, 0:256], shw_up[:, jb, c, :],
                                 xnT[:, c, slab * 256:(slab + 1) * 256],
                                 start=(c == 0), stop=(c == 15)), waits=w)
            up5.append(vm)
            rlu = rows_a[:, pbid * 512:pbid * 512 + 256]
            V(lambda v, pb=pb, rlu=rlu:
              v.tensor_scalar(rlu, pb[:, 0:256], 0.0, None, ALU.max),
              waits=[("T", vm)],
              rd=[pb[:, 0:256]], wr=[rlu])
            h1_ev = V(lambda v, j=j, pb=pb, rlu=rlu:
                      v.tensor_tensor(h1T[:, j, :], rlu, pb[:, 0:256], ALU.mult),
                      rd=[rlu, pb[:, 0:256]], wr=[h1T[:, j, :]])
            pbg5[pbid] = h1_ev
        PBS = [ps_l1[:, 0:512], ps_l1[:, 512:1024], ps_l1[:, 1024:1536], ps_a[:, 0:512],
               ps_l2[:, 0:512], ps_l2[:, 512:1024], ps_l2[:, 1024:1536], ps_b[:, 0:512]]
        if slab == 0:
            dnmm_l = [0, 0]
        for j in range(32):
            jb = j % 2
            w = [("V", h1_ev), ("T", dnmm_l[jb])] + (v_pc_shdn if (slab == 0 and j < 2) else [])
            vwd = p.dma("sync", shw_dn[:, jb, :], dn_s[:, j, :], waits=w)
            for ts in range(2):
                for dch in range(4):
                    gidx5 = ts * 4 + dch
                    w2 = []
                    if gidx5 == 0:
                        w2 = [vwd, ("V", h1_ev)]
                    if j == 0:
                        w2.append(("V", cmb_last))
                    vm = T(lambda t_, j=j, ts=ts, dch=dch, jb=jb, pbv=PBS[gidx5]:
                           t_.matmul(pbv, h1T[:, j, ts * 128:(ts + 1) * 128],
                                     shw_dn[:, jb, dch * 512:(dch + 1) * 512],
                                     start=(j == 0), stop=(j == 31)), waits=w2)
            dnmm_l[jb] = vm
        v_dn = vm
        for ts in range(2):
            tt = slab * 2 + ts
            par = ts
            vrt = p.dma("sync", routed_sb[:, par, :], routed[tt * 128:(tt + 1) * 128, :],
                        waits=[("CC", v_cc_rs), ("V", cmb_last)])
            vsl = p.dma("sync", s_in[:, par, :], s_dram[tt * 128:(tt + 1) * 128, :],
                        waits=v_s_all + [("V", cmb_last)])
            vxo = p.dma("sync", xo_f32[:, par, :], x_own[tt * 128:(tt + 1) * 128, :],
                        waits=[("V", cmb_last)])
            if debug:
                p.dma("sync", dbg["sin_o"][tt * 128:(tt + 1) * 128, :],
                      s_in[:, par, :], waits=[vsl])
                p.dma("sync", dbg["rtd_o"][tt * 128:(tt + 1) * 128, :],
                      routed_sb[:, par, :], waits=[vrt])
            for dch in range(4):
                pbv = PBS[ts * 4 + dch]
                sl = slice(dch * 512, (dch + 1) * 512)
                if debug:
                    vsh = V(lambda v, pbv=pbv, sl=sl:
                            v.tensor_copy(ln_x[:, 0, sl], pbv),
                            waits=[("T", v_dn)] + ([shdump] if shdump else []),
                            rd=[pbv], wr=[ln_x[:, 0, sl]])
                    shdump = p.dma("sync", dbg["shared_o"][tt * 128:(tt + 1) * 128, sl],
                                   ln_x[:, 0, sl], waits=[("V", vsh)])
                vga = V(lambda v, pbv=pbv, par=par, sl=sl:
                        v.tensor_tensor(g_f32[:, par, sl], pbv, routed_sb[:, par, sl],
                                        ALU.add),
                        waits=[("T", v_dn), vrt] +
                        ([out_tok[par]] if out_tok[par] else []),
                        rd=[pbv, routed_sb[:, par, sl]], wr=[g_f32[:, par, sl]])
                vsg = S(lambda s, par=par, sl=sl:
                        s.activation(sg_f32[:, par, sl], g_f32[:, par, sl], ACT.Sigmoid),
                        waits=[("V", vga)],
                        rd=[g_f32[:, par, sl]], wr=[sg_f32[:, par, sl]])
                V(lambda v, par=par, sl=sl:
                  v.tensor_tensor(g_f32[:, par, sl], g_f32[:, par, sl],
                                  sg_f32[:, par, sl], ALU.mult), waits=[("S", vsg)],
                  rd=[g_f32[:, par, sl], sg_f32[:, par, sl]], wr=[g_f32[:, par, sl]])
                V(lambda v, par=par, sl=sl:
                  v.tensor_tensor(g_f32[:, par, sl], g_f32[:, par, sl], s_in[:, par, sl],
                                  ALU.mult), waits=[vsl],
                  rd=[g_f32[:, par, sl], s_in[:, par, sl]], wr=[g_f32[:, par, sl]])
                cmb_last = V(lambda v, par=par, sl=sl:
                             v.tensor_tensor(g_f32[:, par, sl], g_f32[:, par, sl],
                                             xo_f32[:, par, sl], ALU.add),
                             waits=[vxo],
                             rd=[g_f32[:, par, sl], xo_f32[:, par, sl]],
                             wr=[g_f32[:, par, sl]])
            out_tok[par] = p.dma("scalar", out[tt * 128:(tt + 1) * 128, :],
                                 g_f32[:, par, :], waits=[("V", cmb_last)])

    _finish(p, nc)
    nc._prog = p
    return nc, dbg


def _finish(p, nc):
    fin = [(s, v) for s, v in p.cnt.items() if v > 0]
    p.ops.append(("sync", lambda e: e.nop(), tuple(fin), None, 1, None, None))
    import contextlib
    with contextlib.ExitStack() as st:
        for name in p.cnt:
            p.semh[name] = st.enter_context(nc.semaphore(name))
        blk = st.enter_context(nc.Block())

        @blk.sync
        def _(eng):
            p.play("sync", eng)

        @blk.tensor
        def _(eng):
            p.play("tensor", eng)

        @blk.vector
        def _(eng):
            p.play("vector", eng)

        @blk.scalar
        def _(eng):
            p.play("scalar", eng)

        @blk.gpsimd
        def _(eng):
            p.play("gpsimd", eng)


import ml_dtypes

def make_core_inputs(inp, core):
    f32 = np.float32
    x = np.ascontiguousarray(np.asarray(inp["L0_attn_x"], f32).reshape(8192, D))
    chunk = core % 4
    g0 = core * 1024
    x_own = np.ascontiguousarray(x[g0:g0 + 1024])
    if chunk == 0:
        x_halo = np.zeros((1024, D), f32)
    else:
        x_halo = np.ascontiguousarray(x[g0 - 1024:g0])

    qk = np.asarray(inp["L0_attn_q_kernel"], f32)
    wq = np.ascontiguousarray(qk.transpose(1, 0, 2).reshape(D, 2048))
    kvk = np.asarray(inp["L0_attn_kv_kernel"], f32)
    wkv = np.ascontiguousarray(kvk.transpose(2, 1, 0, 3).reshape(D, 2048))
    wo = np.ascontiguousarray(np.asarray(inp["L0_attn_out_kernel"], f32))
    gate_wT = np.ascontiguousarray(np.asarray(inp["L0_ffn_gate_w"], f32).T)
    up = np.asarray(inp["L0_ffn_up_w"], f32)
    dn = np.asarray(inp["L0_ffn_down_w"], f32)
    up_wT = np.ascontiguousarray(up[4 * core:4 * core + 4].transpose(0, 2, 1))
    down_wT = np.ascontiguousarray(dn[4 * core:4 * core + 4].transpose(0, 2, 1))
    sh_upT = np.ascontiguousarray(np.asarray(inp["L0_ffn_shared_up_w"], f32).T)
    sh_dnT = np.ascontiguousarray(np.asarray(inp["L0_ffn_shared_down_w"], f32).T)
    qns = np.ascontiguousarray(np.tile(np.asarray(inp["L0_attn_q_norm_scale"], f32), (128, 1)).astype(ml_dtypes.bfloat16))
    kns = np.ascontiguousarray(np.tile(np.asarray(inp["L0_attn_k_norm_scale"], f32), (128, 1)).astype(ml_dtypes.bfloat16))

    brow = np.zeros((1, 256), f32)
    brow[0, 0:32] = np.asarray(inp["L0_ffn_bias_corr"], f32)
    brow[0, 32:64] = 1e-30 * (np.arange(32) + 1)
    esel = np.zeros((4, 32), f32)
    for e in range(4):
        esel[e, 4 * core + e] = 1.0
    brow[0, 64:192] = esel.reshape(-1)

    frac = np.arange(0, 128, 2, dtype=np.float64) / 128.0
    ts = 10000.0 ** frac

    def rope_tables(tstart, ntiles):
        pos = tstart + np.arange(ntiles * 128).reshape(ntiles, 128)
        pos = np.maximum(pos, 0).astype(np.float64)
        ang = pos[:, :, None] / ts[None, None, :]
        sin = np.sin(ang).astype(ml_dtypes.bfloat16).transpose(1, 0, 2)
        cos = np.cos(ang).astype(ml_dtypes.bfloat16).transpose(1, 0, 2)
        return np.ascontiguousarray(sin), np.ascontiguousarray(cos)

    t0 = 1024 * chunk
    sinq, cosq = rope_tables(t0, 8)
    sink, cosk = rope_tables(t0 - 1024, 16)

    maskin = np.zeros((8, 128, WIN * 128), ml_dtypes.bfloat16)
    iii = np.arange(128)
    for tau in range(8):
        for b in range(WIN):
            blk = np.zeros((128, 128), np.float32)
            if 8 * chunk - 8 + tau + b >= 0:
                if b == 0:
                    blk[:] = (iii[None, :] > iii[:, None])
                elif b == WIN - 1:
                    blk[:] = (iii[None, :] <= iii[:, None])
                else:
                    blk[:] = 1.0
            maskin[tau, :, b * 128:(b + 1) * 128] = blk.astype(ml_dtypes.bfloat16)

    ii = np.arange(128)
    tri_diag = (ii[None, :] <= ii[:, None]).astype(ml_dtypes.bfloat16)
    tri_far = (ii[None, :] > ii[:, None]).astype(ml_dtypes.bfloat16)
    tri_cum = (ii[:, None] < ii[None, :]).astype(ml_dtypes.bfloat16)
    identb = np.eye(128, dtype=ml_dtypes.bfloat16)
    identf = np.eye(128, dtype=f32)
    i4 = (ii[:, None] // 32 == np.arange(4)[None, :])
    return dict(
        x_own=x_own, x_halo=x_halo, wq=wq, wkv=wkv, wo=wo, gate_wT=gate_wT,
        up_wT=up_wT, down_wT=down_wT, sh_upT=sh_upT, sh_dnT=sh_dnT,
        qns=qns, kns=kns, brow_in=brow, sinq=sinq, cosq=cosq, sink=sink, cosk=cosk,
        maskin=maskin, tri_diag=tri_diag, tri_far=tri_far, tri_cum=tri_cum,
        identb=identb, identf=identf,
        i4f=i4.astype(f32), i4b=i4.astype(ml_dtypes.bfloat16),
        ones_f=np.ones((1, 128), f32),
        actc=np.tile(np.array([[1e-6, 1.0 / 128.0, -0.5, np.log(0.08838834764831845),
                                0, 0, 0, 0]], f32), (128, 1)),
        ones_cb=np.ones((128, 1), ml_dtypes.bfloat16),
    )


from concourse.bass_utils import run_bass_kernel_spmd

_NC_CACHE = {}


def kernel(**inputs):
    if "nc" not in _NC_CACHE:
        nc, _ = build_nc(stage=9, debug=0)
        nc.finalize()
        _NC_CACHE["nc"] = nc
    nc = _NC_CACHE["nc"]
    in_maps = [make_core_inputs(inputs, c) for c in range(8)]
    res = run_bass_kernel_spmd(nc, in_maps, core_ids=list(range(8)))
    out = np.concatenate([res.results[c]["out"] for c in range(8)], axis=0)
    return out.reshape(2, 4096, 2048).astype(np.float32)


# version 11
# speedup vs baseline: 1.1310x; 1.1310x over previous
import numpy as np
import concourse.bacc as bacc
import concourse.mybir as mybir
from concourse.library_config import mlp as LIB_MLP, local_scatter as LIB_LS

F32 = mybir.dt.float32
BF16 = mybir.dt.bfloat16
I16 = mybir.dt.int16
ALU = mybir.AluOpType
ACT = mybir.ActivationFunctionType
AXX = mybir.AxisListType.X

D = 2048
CAP = 1024
WIN = 9
ATTN_SCALE = 0.08838834764831845
LN_ATTN = float(np.log(ATTN_SCALE))
RG = [list(range(8))]


_ISZ = {"float32": 4, "bfloat16": 2, "int16": 2, "int32": 4, "uint8": 1,
        "float16": 2, "uint16": 2, "uint32": 4}


def _ext(ap):
    try:
        isz = _ISZ[str(ap.dtype).split(".")[-1]]
        span = 1
        pat = list(ap.ap)
        for pr in pat[1:]:
            span += (pr[1] - 1) * abs(pr[0])
        lo = ap.offset * isz
        return (ap.name, lo, lo + span * isz)
    except Exception:
        return None


class Prog:
    def __init__(self, nc):
        self.nc = nc
        self.ops = []
        self.cnt = {}
        self.semh = {}
        self.qcnt = {}
        self.cnt_play = {}

    def sems(self, names):
        for n in names:
            self.cnt[n] = 0

    def emit(self, eng, fn, waits=(), inc=None, amt=1, rd=None, wr=None):
        ws = tuple((s, v) for (s, v) in waits if v and v > 0)
        if rd is not None:
            rd = [_ext(a) for a in rd]
            if any(x is None for x in rd):
                rd = None
        if wr is not None:
            wr = [_ext(a) for a in wr]
            if any(x is None for x in wr):
                wr = None
        self.ops.append((eng, fn, ws, inc, amt, rd, wr))
        if inc is not None:
            self.cnt[inc] += amt
            return self.cnt[inc]
        return None

    NL = 8

    def _lane(self, eng):
        base = {"sync": "dS", "scalar": "dC", "gpsimd": "dG"}[eng]
        k = self.qcnt.get(eng, 0)
        self.qcnt[eng] = k + 1
        sem = "%s%d" % (base, k % self.NL)
        if sem not in self.cnt:
            self.cnt[sem] = 0
        return sem

    def dma(self, eng, out, in_, waits=()):
        sem = self._lane(eng)
        ws = list(waits) + ([(sem, self.cnt[sem])] if self.cnt[sem] else [])
        v = self.emit(eng, (lambda e, o=out, i=in_: e.dma_start(out=o, in_=i)),
                      waits=ws, inc=sem, amt=16)
        return (sem, v)

    def dmalike(self, eng, fn, waits=(), base=None):
        if base is None:
            sem = self._lane(eng)
        else:
            k = self.qcnt.get(base, 0)
            self.qcnt[base] = k + 1
            sem = "%s%d" % (base, k % self.NL)
            if sem not in self.cnt:
                self.cnt[sem] = 0
        ws = list(waits) + ([(sem, self.cnt[sem])] if self.cnt[sem] else [])
        v = self.emit(eng, fn, waits=ws, inc=sem, amt=16)
        return (sem, v)

    def qbar(self, eng):
        base = {"sync": "dS", "scalar": "dC", "gpsimd": "dG"}[eng]
        return [(sem, v) for sem, v in self.cnt.items()
                if sem.startswith(base) and v > 0]

    def val(self, sem):
        return self.cnt[sem]

    def play(self, eng_name, eng):
        own = {"vector": "V", "scalar": "S", "gpsimd": "G"}.get(eng_name)
        ew = {}
        by_name = {}
        last_all = 0
        myval = 0
        for (e, fn, waits, inc, amt, rd, wr) in self.ops:
            if e != eng_name:
                continue
            if own is not None and inc == own:
                if rd is None:
                    need = myval
                else:
                    need = last_all
                    for (name, lo, hi) in rd:
                        for (val, l2, h2) in reversed(by_name.get(name, ())):
                            if val <= need:
                                break
                            if l2 < hi and lo < h2:
                                need = val
                                break
                if need > 0 and ew.get(own, 0) < need:
                    eng.wait_ge(self.semh[own], need)
                    ew[own] = need
            for (s, v) in waits:
                if ew.get(s, 0) < v:
                    eng.wait_ge(self.semh[s], v)
                    ew[s] = v
            inst = fn(eng)
            if inc is not None:
                inst.then_inc(self.semh[inc], amt)
                if own is not None and inc == own:
                    myval += amt
                    if wr is None:
                        last_all = myval
                    else:
                        for (name, lo, hi) in wr:
                            by_name.setdefault(name, []).append((myval, lo, hi))


def build_nc(stage=9, debug=0):
    nc = bacc.Bacc("TRN2", num_swdge_queues=2)

    def din(name, shape, dt):
        return nc.dram_tensor(name, shape, dt, kind="ExternalInput").ap()

    def dout(name, shape, dt):
        return nc.dram_tensor(name, shape, dt, kind="ExternalOutput").ap()

    def dint(name, shape, dt):
        return nc.dram_tensor(name, shape, dt, kind="Internal").ap()

    x_own = din("x_own", [1024, D], F32)
    x_halo = din("x_halo", [1024, D], F32)
    wq = din("wq", [D, 2048], F32)
    wkv = din("wkv", [D, 2048], F32)
    wo = din("wo", [16, 128, D], F32)
    gate_wT = din("gate_wT", [D, 32], F32)
    up_wT = din("up_wT", [4, D, 1024], F32)
    down_wT = din("down_wT", [4, 1024, D], F32)
    sh_upT = din("sh_upT", [D, 4096], F32)
    sh_dnT = din("sh_dnT", [4096, D], F32)
    qns = din("qns", [128, 128], BF16)
    kns = din("kns", [128, 128], BF16)
    brow_in = din("brow_in", [1, 256], F32)
    actc = din("actc", [128, 8], F32)
    sinq = din("sinq", [128, 8, 64], BF16)
    cosq = din("cosq", [128, 8, 64], BF16)
    sink = din("sink", [128, 16, 64], BF16)
    cosk = din("cosk", [128, 16, 64], BF16)
    maskin = din("maskin", [8, 128, WIN * 128], BF16)
    tri_diag = din("tri_diag", [128, 128], BF16)
    tri_far = din("tri_far", [128, 128], BF16)
    tri_cum = din("tri_cum", [128, 128], BF16)
    identb = din("identb", [128, 128], BF16)
    identf = din("identf", [128, 128], F32)
    i4f = din("i4f", [128, 4], F32)
    i4b = din("i4b", [128, 4], BF16)
    ones_f = din("ones_f", [1, 128], F32)
    ones_cb = din("ones_cb", [128, 1], BF16)

    out = dout("out", [1024, D], F32)

    ag_sc_in = dint("ag_sc_in", [1024, 32], F32)
    scores_all = dint("scores_all", [8192, 32], F32)
    ag_xn_in = dint("ag_xn_in", [1024, D], BF16)
    xn_all = dint("xn_all", [8192, D], BF16)
    st_dump = dint("st_dump", [128, 4, 64], I16)
    whi_dump = dint("whi_dump", [128, 4, 64], BF16)
    wlo_dump = dint("wlo_dump", [128, 4, 64], BF16)
    gidx_dump = dint("gidx_dump", [4, CAP], I16)
    sidx_dump = dint("sidx_dump", [4, CAP], I16)
    partial = dint("partial", [8320, D], BF16)
    routed = dint("routed", [1024, D], BF16)
    s_dram = dint("s_dram", [1024, D], BF16)
    wq_bf = dint("wq_bf", [D, 2048], BF16)
    wkv_bf = dint("wkv_bf", [D, 2048], BF16)
    wo_bf = dint("wo_bf", [16, 128, D], BF16)
    shup_bf = dint("shup_bf", [D, 4096], BF16)
    shdn_bf = dint("shdn_bf", [4096, D], BF16)

    dbg = {}
    if debug:
        dbg["xn_all_o"] = dout("xn_all_o", [8192, D], BF16)
        dbg["scores_o"] = dout("scores_o", [8192, 32], F32)
        dbg["gidx_o"] = dout("gidx_o", [4, CAP], I16)
        dbg["sidx_o"] = dout("sidx_o", [4, CAP], I16)
        dbg["wslot_o"] = dout("wslot_o", [128, 4, 8], F32)
        dbg["routed_o"] = dout("routed_o", [1024, D], BF16)
        dbg["s_o"] = dout("s_o", [1024, D], BF16)
        dbg["shared_o"] = dout("shared_o", [1024, D], F32)
        dbg["sin_o"] = dout("sin_o", [1024, D], BF16)
        dbg["rtd_o"] = dout("rtd_o", [1024, D], BF16)

    stack = []

    def sb(name, shape, dt):
        t = nc.sbuf_tensor(name, shape, dt)
        stack.append(t)
        return t.__enter__()

    def psum(name, shape, dt=F32):
        t = nc.psum_tensor(name, shape, dt)
        stack.append(t)
        return t.__enter__()

    xnT = sb("xnT", [128, 16, 1024], BF16)
    AR1 = sb("AR1", [128, 24576], BF16)
    AR2 = sb("AR2", [128, 8192], BF16)
    AR3 = sb("AR3", [128, 18432], BF16)
    wst = sb("wst", [128, 16, 512], BF16)
    ln_x = sb("ln_x", [128, 2, D], F32)
    xn_rows = sb("xn_rows", [128, 2, D], BF16)
    rows_a = sb("rows_a", [128, 1024], F32)
    rows_b = sb("rows_b", [128, 1024], F32)
    rms_t = sb("rms_t", [128, 1024], F32)
    rms_s = sb("rms_s", [128, 16], F32)
    ln_stat = sb("ln_stat", [128, 32], F32)
    rope_q = sb("rope_q", [128, 2, 8, 64], BF16)
    rope_k = sb("rope_k", [128, 2, 16, 64], BF16)
    scale_q = sb("scale_q", [128, 128], BF16)
    scale_k = sb("scale_k", [128, 128], BF16)
    identb_sb = sb("identb_sb", [128, 128], BF16)
    identf_sb = sb("identf_sb", [128, 128], F32)
    tri_cum_sb = sb("tri_cum_sb", [128, 128], BF16)
    tri_diag_sb = sb("tri_diag_sb", [128, 128], BF16)
    tri_far_sb = sb("tri_far_sb", [128, 128], BF16)
    ones_f_sb = sb("ones_f_sb", [1, 128], F32)
    ones_cb_sb = sb("ones_cb_sb", [128, 1], BF16)
    i4f_sb = sb("i4f_sb", [128, 4], F32)
    i4b_sb = sb("i4b_sb", [128, 4], BF16)

    brow_sb = sb("brow_sb", [1, 256], F32)
    actc_sb = sb("actc_sb", [128, 8], F32)
    gwT_sb = sb("gwT_sb", [128, 16, 32], BF16)
    bias_t = sb("bias_t", [128, 32], F32)
    epse_t = sb("epse_t", [128, 32], F32)
    esel_t = sb("esel_t", [128, 4, 32], F32)
    m_max = sb("m_max", [128, 64], F32)
    wsum = sb("wsum", [128, 64], F32)
    wslot = sb("wslot", [128, 4, 8], F32)
    gidx_rep = sb("gidx_rep", [128, CAP // 16], I16)
    sidx_rep = sb("sidx_rep", [128, CAP // 16], I16)
    negmax = sb("negmax", [128, 2], F32)
    dsum = sb("dsum", [128, 2], F32)
    s_stage = sb("s_stage", [128, 2, D], BF16)

    qT = AR1[:, 0:8192].rearrange("p (a t) -> p a t", a=8)
    kT = AR1[:, 8192:16384].rearrange("p (a t) -> p a t", a=4)
    v_sb = AR1[:, 16384:24576].rearrange("p (t a h) -> p t a h", t=16, a=4)
    halo_ring = AR2[:, 0:8192].rearrange("p (c t) -> p c t", c=16)
    probs = AR2[:, 0:2304].rearrange("p (a t) -> p a t", a=2)
    maskt = AR2[:, 2304:3456]
    encT = AR2[:, 3456:5504].rearrange("p (a b h) -> p a b h", a=8, b=2)
    wo_sb = AR2[:, 5504:6528].rearrange("p (a h) -> p a h", a=2)
    probsT = AR2[:, 6528:7680].rearrange("p (a h) -> p a h", a=WIN)
    AR3f = AR3.bitcast(F32)
    sc_all = AR3f[:, 0:2048].rearrange("p (t e) -> p t e", t=64)
    sel_t = AR3f[:, 2048:4096].rearrange("p (t e) -> p t e", t=64)
    selr = AR3f[:, 4096:6144].rearrange("p (t e) -> p t e", t=64)
    pos_t = AR3f[:, 6144:8192].rearrange("p (t e) -> p t e", t=64)
    sel_bf = AR3[:, 16384:18432].rearrange("p (t e) -> p t e", t=64)
    xeT2 = [AR1[:, 0:8192].rearrange("p (c t) -> p c t", c=16),
            AR1[:, 8192:16384].rearrange("p (c t) -> p c t", c=16)]
    hT = AR1[:, 16384:24576].rearrange("p (j t) -> p j t", j=8)
    yew = AR3[:, 0:16384].rearrange("p (a d) -> p a d", a=8)
    dw_sb = AR2[:, 0:8192].rearrange("p (b j n) -> p b j n", b=2, j=8)
    h1T = AR1[:, 0:8192].rearrange("p (j t) -> p j t", j=32)
    g_f32 = AR1[:, 8192:16384].bitcast(F32).rearrange("p (a d) -> p a d", a=2)
    xo_f32 = AR1[:, 16384:24576].bitcast(F32).rearrange("p (a d) -> p a d", a=2)
    sg_f32 = AR3.bitcast(F32)[:, 0:4096].rearrange("p (a d) -> p a d", a=2)
    routed_sb = AR3[:, 8192:12288].rearrange("p (a d) -> p a d", a=2)
    s_in = AR3[:, 12288:16384].rearrange("p (a d) -> p a d", a=2)
    shw_up = AR2[:, 0:8192].rearrange("p (b c h) -> p b c h", b=4, c=16)
    shw_dn = AR3[:, 0:4096].rearrange("p (b n) -> p b n", b=2)
    zero_v = AR1[:, 0:2048]
    AR2f = AR2.bitcast(F32)
    cnt_t = AR2f[0:1, 0:2048].rearrange("p (t e) -> p t e", t=64)
    off_t = AR2f[0:1, 2048:4096].rearrange("p (t e) -> p t e", t=64)
    sigT = AR3.bitcast(F32)[0:32, 0:1024]
    sc_rows = AR3.bitcast(F32)[:, 1024:1280].rearrange("p (t e) -> p t e", t=8)
    wstf = wst.rearrange("p a b -> p (a b)").bitcast(F32)
    mrg = wstf[0:4, 0:3072]
    xnrf = xn_rows.rearrange("p a d -> p (a d)").bitcast(F32)
    r_a = xnrf[:, 0:512].rearrange("p (t e) -> p t e", t=64)
    r_b = xnrf[:, 512:1024].rearrange("p (t e) -> p t e", t=64)
    r_c = xnrf[:, 1024:1536].rearrange("p (t e) -> p t e", t=64)
    r_d = xnrf[:, 1536:2048].rearrange("p (t e) -> p t e", t=64)
    rank_t = rows_a[:, 0:512].rearrange("p (t e) -> p t e", t=64)
    selg = rows_a[:, 512:1024].rearrange("p (t e) -> p t e", t=64)
    lso_tok_f = ln_x[:, 0, 0:1024]
    own3 = ln_x[:, 1, 0:768].rearrange("p (k t) -> p k t", k=12)
    own_i = ln_x[:, 1, 768:896].bitcast(I16).rearrange("p (e t) -> p e t", e=4)
    own_hi = ln_x[:, 1, 896:1024].bitcast(BF16).rearrange("p (e t) -> p e t", e=4)
    own_lo = ln_x[:, 1, 1024:1152].bitcast(BF16).rearrange("p (e t) -> p e t", e=4)
    relu_f = rows_a[:, 0:512]
    rbi = rows_b.bitcast(I16)
    rbb = rows_b.bitcast(BF16)
    ls_idx = rbi[:, 1024:1280]
    ls_tok = rbi[:, 1280:1536]
    ls_whi = rbb[:, 1536:1792]
    ls_wlo = rbb[:, 1792:2048]
    wsti = wst.rearrange("p a b -> p (a b)").bitcast(I16)
    gidx_i = wsti[0:4, 6144:7168]
    sidx_i = wsti[0:4, 7168:8192]
    lso_tok = rows_b.bitcast(I16)[:, 0:1024]
    lso_whi = rms_t.bitcast(BF16)[:, 0:1024]
    lso_wlo = rms_t.bitcast(BF16)[:, 1024:2048]

    ps_l1 = psum("ps_l1", [128, 1536], F32)
    ps_l2 = psum("ps_l2", [128, 1536], F32)
    ps_a = psum("ps_a", [128, 512], F32)
    ps_b = psum("ps_b", [128, 512], F32)

    p = Prog(nc)
    p.sems(["T", "V", "S", "G", "CC"])

    def T(fn, waits=()):
        return p.emit("tensor", fn, waits, "T")

    def V(fn, waits=(), rd=None, wr=None):
        return p.emit("vector", fn, waits, "V", rd=rd, wr=wr)

    def S(fn, waits=(), rd=None, wr=None):
        return p.emit("scalar", fn, waits, "S", rd=rd, wr=wr)

    def G(fn, waits=(), rd=None, wr=None):
        return p.emit("gpsimd", fn, waits, "G", rd=rd, wr=wr)

    def Gd(fn, waits=()):
        return p.dmalike("gpsimd", fn, waits, base="dX")

    def CCe(fn, waits=()):
        return p.emit("gpsimd", fn, waits, "CC", 1)

    def bar(*engs):
        return [(e, p.val(e)) for e in engs]

    for dst, src in [(identb_sb, identb), (identf_sb, identf), (tri_cum_sb, tri_cum),
                     (tri_diag_sb, tri_diag), (tri_far_sb, tri_far),
                     (ones_f_sb, ones_f), (ones_cb_sb, ones_cb), (i4f_sb, i4f),
                     (i4b_sb, i4b), (scale_q, qns),
                     (scale_k, kns), (brow_sb, brow_in), (actc_sb, actc)]:
        p.dma("sync", dst[:, :], src[:, :])
    p.dma("sync", rope_q[:, 0, :, :], sinq[:, :, :])
    p.dma("sync", rope_q[:, 1, :, :], cosq[:, :, :])
    p.dma("sync", rope_k[:, 0, :, :], sink[:, :, :])
    p.dma("sync", rope_k[:, 1, :, :], cosk[:, :, :])
    v_consts = p.qbar("sync")

    if stage < 0.22:
        _finish(p, nc)
        return nc, dbg
    if stage < 0.24:
        _finish(p, nc)
        return nc, dbg
    p.dma("gpsimd", gwT_sb[:, :, :], gate_wT.rearrange("(c p) n -> p c n", p=128))
    v_pc_gw = p.qbar("gpsimd")
    for (d_, s_) in [(wkv_bf, wkv), (wq_bf, wq)]:
        for h in range(2):
            p.dma("gpsimd", d_.rearrange("(a p) n -> a p n", p=128)[h * 8:(h + 1) * 8],
                  s_.rearrange("(a p) n -> a p n", p=128)[h * 8:(h + 1) * 8])
        if d_ is wkv_bf:
            v_pc_kv = p.qbar("gpsimd")
    v_pc_q = p.qbar("gpsimd")
    p.dma("gpsimd", wo_bf.rearrange("n p d -> p n d"), wo.rearrange("n p d -> p n d"))
    v_pc_wo = p.qbar("gpsimd")
    for h in range(2):
        p.dma("gpsimd", shup_bf.rearrange("(a p) n -> a p n", p=128)[h * 8:(h + 1) * 8],
              sh_upT.rearrange("(a p) n -> a p n", p=128)[h * 8:(h + 1) * 8])
    v_pc_shup = p.qbar("gpsimd")
    for h in range(4):
        p.dma("gpsimd", shdn_bf.rearrange("(a p) n -> a p n", p=128)[h * 8:(h + 1) * 8],
              sh_dnT.rearrange("(a p) n -> a p n", p=128)[h * 8:(h + 1) * 8])
    v_pc_shdn = p.qbar("gpsimd")
    v_precast = p.qbar("gpsimd")

    if stage < 0.25:
        _finish(p, nc)
        return nc, dbg
    vz = G(lambda g: g.memset(zero_v, 0.0), wr=[zero_v])

    if stage < 0.3:
        _finish(p, nc)
        return nc, dbg
    vb = T(lambda t: t.matmul(ps_b[:, 0:256], ones_f_sb[:1, :], brow_sb[:1, :],
                              start=True, stop=True), waits=v_consts)
    V(lambda v: v.tensor_copy(bias_t[:, :], ps_b[:, 0:32]), waits=[("T", vb)],
      rd=[ps_b[:, 0:32]], wr=[bias_t[:, :]])
    V(lambda v: v.tensor_copy(epse_t[:, :], ps_b[:, 32:64]),
      rd=[ps_b[:, 32:64]], wr=[epse_t[:, :]])
    v_bias = V(lambda v: v.tensor_copy(esel_t.rearrange("p a e -> p (a e)"),
                                       ps_b[:, 64:192]),
               rd=[ps_b[:, 64:192]], wr=[esel_t[:, :, :]])

    if stage < 0.4:
        _finish(p, nc)
        return nc, dbg
    LNE = 1e-5
    ln_guard = [0, 0]
    xnr_guard = [[], []]

    ln_dguard = [[]]

    def emit_ln(src_rows, slot):
        vw = p.dma("sync", ln_x[:, slot, :], src_rows,
                   waits=[("V", ln_guard[slot])] + ln_dguard[0])
        vbn = 0
        for j in range(4):
            vbn = V(lambda v, j=j, slot=slot: v.bn_stats(ln_stat[:, j * 6:(j + 1) * 6],
                                                         ln_x[:, slot, j * 512:(j + 1) * 512]),
                    waits=[vw] if j == 0 else (),
                    rd=[ln_x[:, slot, j * 512:(j + 1) * 512]],
                    wr=[ln_stat[:, j * 6:(j + 1) * 6]])
        vag = V(lambda v: v.bn_aggr(ln_stat[:, 24:26], ln_stat[:, 0:24]),
                waits=[("V", vbn)],
                rd=[ln_stat[:, 0:24]], wr=[ln_stat[:, 24:26]])
        vv = V(lambda v: v.tensor_scalar(ln_stat[:, 26:27], ln_stat[:, 25:26], LNE,
                                         None, ALU.add), waits=[("V", vag)],
               rd=[ln_stat[:, 25:26]], wr=[ln_stat[:, 26:27]])
        S(lambda s: s.activation(ln_stat[:, 27:28], ln_stat[:, 26:27], ACT.Ln),
          waits=[("V", vv)] + v_consts,
          rd=[ln_stat[:, 26:27]], wr=[ln_stat[:, 27:28]])
        vs = S(lambda s: s.activation(ln_stat[:, 28:29], ln_stat[:, 27:28], ACT.Exp,
                                      scale=actc_sb[:, 2:3]),
               rd=[ln_stat[:, 27:28], actc_sb[:, 2:3]], wr=[ln_stat[:, 28:29]])
        vout = V(lambda v, slot=slot: v.tensor_scalar(xn_rows[:, slot, :],
                                                      ln_x[:, slot, :],
                                                      ln_stat[:, 24:25], ln_stat[:, 28:29],
                                                      ALU.subtract, ALU.mult),
                 waits=[("S", vs)] + xnr_guard[slot],
                 rd=[ln_x[:, slot, :], ln_stat[:, 24:25], ln_stat[:, 28:29]],
                 wr=[xn_rows[:, slot, :]])
        ln_guard[slot] = vout
        return vout

    def emit_xpose16(slot, vln, dst_fn, extra0=()):
        evs = []
        for c in range(16):
            pp = ps_a if c % 2 == 0 else ps_b
            w = [("V", vln)] if c == 0 else []
            if c < 2:
                w += list(extra0) + v_consts
            if c >= 2:
                w.append(("S", evs[c - 2]))
            vt = T(lambda t_, c=c, slot=slot, pp=pp:
                   t_.transpose(pp.bitcast(BF16)[:, 0:128],
                                xn_rows[:, slot, c * 128:(c + 1) * 128], identb_sb[:, :]),
                   waits=w)
            evs.append(S(lambda s, c=c, pp=pp, dst=dst_fn(c):
                         s.copy(out=dst, in_=pp.bitcast(BF16)[:, 0:128]),
                         waits=[("T", vt)],
                         rd=[pp.bitcast(BF16)[:, 0:128]], wr=[dst_fn(c)]))
        return evs

    for t in range(8 if stage >= 0.45 else 1):
        slot = t % 2
        vln = emit_ln(x_own[t * 128:(t + 1) * 128, :], slot)
        vdump = p.dma("scalar", ag_xn_in[t * 128:(t + 1) * 128, :], xn_rows[:, slot, :],
                      waits=[("V", vln)])
        evs = emit_xpose16(slot, vln, lambda c, t=t: xnT[:, c, t * 128:(t + 1) * 128],
                           extra0=([("V", v_bias)] if t == 0 else ()))
        xnr_guard[slot] = [vdump, ("S", evs[-1])]
    v_xnT = p.val("S")

    if stage < 0.7:
        _finish(p, nc)
        return nc, dbg
    v_agxn = p.qbar("scalar")
    CCe(lambda g: g.collective_compute("AllGather", ALU.bypass, RG, [ag_xn_in], [xn_all]),
        waits=v_agxn)
    v_cc_xn = p.val("CC")

    for i in range(65):
        p.dma("gpsimd", partial.rearrange("(a p) d -> a p d", p=128)[i], zero_v,
              waits=[("G", vz)] if i == 0 else ())
    v_pzero = p.qbar("gpsimd")

    if stage < 0.8:
        _finish(p, nc)
        return nc, dbg
    v_sigp = 0
    for t in range(8):
        for c in range(16):
            w = ([("S", v_xnT), ("V", v_bias)] + v_pc_gw) if (t == 0 and c == 0) else []
            if c == 0 and t > 0:
                w = [("S", v_sigp)]
            T(lambda t_, c=c, t=t: t_.matmul(ps_b[0:32, 0:128], gwT_sb[:, c, :],
                                             xnT[:, c, t * 128:(t + 1) * 128],
                                             start=(c == 0), stop=(c == 15)), waits=w)
        vq = p.val("T")
        v_sigp = S(lambda s, t=t: s.activation(sigT[:, t * 128:(t + 1) * 128], ps_b[0:32, 0:128],
                                      ACT.Sigmoid), waits=[("T", vq)],
                   rd=[ps_b[0:32, 0:128]], wr=[sigT[:, t * 128:(t + 1) * 128]])
    v_sig = p.val("S")
    for t in range(8):
        w = ([("S", v_sig)] + v_consts) if t == 0 else [("V", p.val("V"))]
        vt = T(lambda t_, t=t: t_.transpose(ps_a[:, 0:32],
                                            sigT[0:32, t * 128:(t + 1) * 128],
                                            identf_sb[0:32, 0:32]), waits=w)
        V(lambda v, t=t: v.tensor_copy(sc_rows[:, t, :], ps_a[:, 0:32]), waits=[("T", vt)],
          rd=[ps_a[:, 0:32]], wr=[sc_rows[:, t, :]])
    v_scr = p.val("V")
    v_scd = p.dma("scalar", ag_sc_in.rearrange("(t p) e -> p t e", p=128),
                  sc_rows[:, :, :], waits=[("V", v_scr)])
    CCe(lambda g: g.collective_compute("AllGather", ALU.bypass, RG, [ag_sc_in],
                                       [scores_all]), waits=[v_scd])
    v_cc_sc = p.val("CC")

    if debug:
        p.dma("sync", dbg["xn_all_o"][:, :], xn_all[:, :], waits=[("CC", v_cc_xn)])
        p.dma("sync", dbg["scores_o"][:, :], scores_all[:, :], waits=[("CC", v_cc_sc)])

    if stage < 2:
        _finish(p, nc)
        return nc, dbg

    st_ = {"ktr": 0, "kv_ev": [0, 0], "q_ev": [0, 0], "s_ld": {}, "tp_ev": [0, 0]}

    def attn_pass(PS):
        exp_l, ve2_l, enc_l = [], [], []
        op_guard = []
        stamp = p.val("V")
        st_["kv_ev"] = [stamp, stamp]
        st_["q_ev"] = [stamp, stamp]
        ln_guard[0] = max(ln_guard[0], stamp)
        ln_guard[1] = max(ln_guard[1], stamp)
        for half in range(2):
            vw_kv = p.dma("sync", wst[:, :, :],
                          wkv_bf.rearrange("(c p) n -> p c n", p=128)
                          [:, :, PS * 1024 + half * 512:PS * 1024 + (half + 1) * 512],
                          waits=bar("T", "V") + v_pc_kv)
            for ti in range(16):
                v_ring = 0
                if ti < 8:
                    slot = ti % 2
                    vln = emit_ln(x_halo[ti * 128:(ti + 1) * 128, :], slot)
                    rc = (ti % 4) * 128
                    evs = emit_xpose16(slot, vln,
                                       lambda c, rc=rc: halo_ring[:, c, rc:rc + 128],
                                       extra0=bar("S"))
                    xnr_guard[slot] = [("S", evs[-1])]
                    v_ring = evs[-1]
                pb = ps_l1 if ti % 2 == 0 else ps_l2
                for c in range(16):
                    w = []
                    if c == 0:
                        w = [vw_kv, ("V", st_["kv_ev"][ti % 2])]
                        w.append(("S", v_ring if ti < 8 else v_xnT))
                    src = (halo_ring[:, c, (ti % 4) * 128:(ti % 4) * 128 + 128] if ti < 8
                           else xnT[:, c, (ti - 8) * 128:(ti - 8) * 128 + 128])
                    T(lambda t_, c=c, src=src, pb=pb:
                      t_.matmul(pb[:, 0:512], src, wst[:, c, :],
                                start=(c == 0), stop=(c == 15)), waits=w)
                vkvm = p.val("T")
                for i2 in range(2):
                    kvh = half * 2 + i2
                    V(lambda v, i2=i2, pb=pb:
                      v.tensor_copy(rows_a[:, i2 * 128:(i2 + 1) * 128],
                                    pb[:, i2 * 256:i2 * 256 + 128]),
                      waits=[("T", vkvm), ("T", st_["ktr"])] if i2 == 0 else (),
                      rd=[pb[:, i2 * 256:i2 * 256 + 128]],
                      wr=[rows_a[:, i2 * 128:(i2 + 1) * 128]])
                    V(lambda v, i2=i2, kvh=kvh, ti=ti, pb=pb:
                      v.tensor_copy(v_sb[:, ti, kvh, :],
                                    pb[:, i2 * 256 + 128:i2 * 256 + 256]),
                      rd=[pb[:, i2 * 256 + 128:i2 * 256 + 256]],
                      wr=[v_sb[:, ti, kvh, :]])
                nh = 2
                _rms_rope(nh, rope_k, ti, scale_k, False)
                vv = p.val("V")
                st_["kv_ev"][ti % 2] = vv
                for i2 in range(2):
                    kvh = half * 2 + i2
                    pp = ps_a if i2 % 2 == 0 else ps_b
                    w = ([("V", vv)] + v_consts) if i2 == 0 else []
                    w.append(("V", st_["tp_ev"][i2 % 2]))
                    vt = T(lambda t_, i2=i2, pp=pp:
                           t_.transpose(pp[:, 0:128], rows_b[:, i2 * 128:(i2 + 1) * 128],
                                        identf_sb[:, :]), waits=w)
                    st_["ktr"] = vt
                    st_["tp_ev"][i2 % 2] = V(lambda v, kvh=kvh, ti=ti, pp=pp:
                      v.tensor_copy(kT[:, kvh, ti * 128:(ti + 1) * 128], pp[:, 0:128]),
                      waits=[("T", vt)],
                      rd=[pp[:, 0:128]], wr=[kT[:, kvh, ti * 128:(ti + 1) * 128]])
        for half in range(2):
            vw_q = p.dma("sync", wst[:, :, :],
                         wq_bf.rearrange("(c p) n -> p c n", p=128)
                         [:, :, PS * 1024 + half * 512:PS * 1024 + (half + 1) * 512],
                         waits=bar("T", "V") + v_pc_q)
            for ti in range(8):
                pb = ps_l1 if ti % 2 == 0 else ps_l2
                for c in range(16):
                    w = [vw_q, ("V", st_["q_ev"][ti % 2])] if c == 0 else []
                    T(lambda t_, c=c, ti=ti, pb=pb:
                      t_.matmul(pb[:, 0:512], xnT[:, c, ti * 128:(ti + 1) * 128],
                                wst[:, c, :], start=(c == 0), stop=(c == 15)), waits=w)
                vqm = p.val("T")
                V(lambda v, pb=pb: v.tensor_copy(rows_a[:, 0:512], pb[:, 0:512]),
                  waits=[("T", vqm), ("T", st_["ktr"])],
                  rd=[pb[:, 0:512]], wr=[rows_a[:, 0:512]])
                _rms_rope(4, rope_q, ti, scale_q, True)
                vv = p.val("V")
                st_["q_ev"][ti % 2] = vv
                for i2 in range(4):
                    qh = half * 4 + i2
                    pp = ps_a if i2 % 2 == 0 else ps_b
                    w = ([("V", vv)] + v_consts) if i2 == 0 else []
                    w.append(("V", st_["tp_ev"][i2 % 2]))
                    vt = T(lambda t_, i2=i2, pp=pp:
                           t_.transpose(pp[:, 0:128], rows_b[:, i2 * 128:(i2 + 1) * 128],
                                        identf_sb[:, :]), waits=w)
                    st_["ktr"] = vt
                    st_["tp_ev"][i2 % 2] = V(lambda v, qh=qh, ti=ti, pp=pp:
                      v.tensor_copy(qT[:, qh, ti * 128:(ti + 1) * 128], pp[:, 0:128]),
                      waits=[("T", vt)],
                      rd=[pp[:, 0:128]], wr=[qT[:, qh, ti * 128:(ti + 1) * 128]])
        mask_ev = {}
        ve1_l = []
        stA = {}

        def emit_AT(gi):
            tau, it = divmod(gi, 8)
            kvh, gg = divmod(it, 2)
            qh = kvh * 2 + gg
            par = gi % 2
            pb = ps_l1 if par == 0 else ps_l2
            if it == 0:
                wm = bar("V", "S")
                mask_ev[tau] = p.dma("sync", maskt[:, :], maskin[tau], waits=wm)
            v_mask = mask_ev[tau]
            w0 = [v_mask]
            if gi >= 2:
                w0.append(("S", exp_l[gi - 2]))
            for ci, (c0, cn) in enumerate([(0, 512), (512, 512), (1024, 128)]):
                T(lambda t_, tau=tau, kvh=kvh, qh=qh, c0=c0, cn=cn, pb=pb:
                  t_.matmul(pb[:, c0:c0 + cn], qT[:, qh, tau * 128:(tau + 1) * 128],
                            kT[:, kvh, tau * 128 + c0:tau * 128 + c0 + cn],
                            start=True, stop=True), waits=w0 if ci == 0 else ())
                w0 = []
            vlg = p.val("T")
            vmx = V(lambda v, par=par, pb=pb:
                    v.tensor_reduce(negmax[:, par:par + 1], pb[:, 0:1152], AXX,
                                    ALU.max, negate=True), waits=[("T", vlg)],
                    rd=[pb[:, 0:1152]], wr=[negmax[:, par:par + 1]])
            stA[gi] = [tau, it, kvh, qh, par, pb, v_mask, vmx, None]

        def emit_AS(gi):
            tau, it, kvh, qh, par, pb, v_mask, vmx, _ = stA[gi]
            vex = S(lambda s, par=par, pb=pb:
                    s.activation(probs[:, par, :], pb[:, 0:1152], ACT.Exp,
                                 bias=negmax[:, par:par + 1]), waits=[("V", vmx)],
                    rd=[pb[:, 0:1152], negmax[:, par:par + 1]], wr=[probs[:, par, :]])
            exp_l.append(vex)
            V(lambda v, par=par:
              v.scalar_tensor_tensor(probs[:, par, :], probs[:, par, :], 1.0,
                                     maskt[:, :], ALU.mult, ALU.mult,
                                     accum_out=dsum[:, par:par + 1]),
              waits=[("S", vex), v_mask],
              rd=[probs[:, par, :], maskt[:, :]],
              wr=[probs[:, par, :], dsum[:, par:par + 1]])
            V(lambda v, par=par: v.reciprocal(dsum[:, par:par + 1],
                                              dsum[:, par:par + 1]),
              rd=[dsum[:, par:par + 1]], wr=[dsum[:, par:par + 1]])
            vnrm = V(lambda v, par=par:
                     v.tensor_scalar(probs[:, par, :], probs[:, par, :],
                                     dsum[:, par:par + 1], None, ALU.mult),
                     rd=[probs[:, par, :], dsum[:, par:par + 1]],
                     wr=[probs[:, par, :]])
            stA[gi][8] = vnrm

        def emit_B1a(gi):
            nonlocal op_guard
            tau, it, kvh, qh, par, pb, v_mask, vmx, vnrm = stA[gi]
            for c in range(8):
                w = []
                if c == 0:
                    w = [("V", vnrm)] + op_guard
                    if ve1_l:
                        w.append(("S", ve1_l[-1]))
                    op_guard = []
                T(lambda t_, c=c, par=par:
                  t_.transpose(ps_a.bitcast(BF16)[:, c * 128:(c + 1) * 128],
                               probs[:, par, c * 128:(c + 1) * 128], identb_sb[:, :]),
                  waits=w)
            vt9 = T(lambda t_, par=par, pb=pb:
                    t_.transpose(pb.bitcast(BF16)[:, 2304:2432],
                                 probs[:, par, 1024:1152], identb_sb[:, :]))
            stA[gi].append(vt9)

        def emit_B1b(gi):
            vt9 = stA[gi][9]
            par = stA[gi][4]
            pb = stA[gi][5]
            ve1 = S(lambda s: s.copy(out=probsT[:, 0:8, :].rearrange("p a b -> p (a b)"),
                                     in_=ps_a.bitcast(BF16)[:, 0:1024]),
                    waits=[("T", vt9)],
                    rd=[ps_a.bitcast(BF16)[:, 0:1024]],
                    wr=[probsT[:, 0:8, :]])
            ve1_l.append(ve1)
            ve2 = S(lambda s, pb=pb: s.copy(out=probsT[:, 8, :],
                                            in_=pb.bitcast(BF16)[:, 2304:2432]),
                    waits=[("T", vt9)],
                    rd=[pb.bitcast(BF16)[:, 2304:2432]], wr=[probsT[:, 8, :]])
            ve2_l.append(ve2)

        def emit_B2(gi):
            tau, it, kvh, qh, par, pb = stA.pop(gi)[0:6]
            ve1 = ve1_l[gi]
            ve2 = ve2_l[gi]
            for c in range(WIN):
                w = []
                if c == 0:
                    w = [("S", ve2)]
                    if enc_l:
                        w.append(("V", enc_l[-1]))
                vav = T(lambda t_, c=c, tau=tau, kvh=kvh:
                        t_.matmul(ps_b[:, 0:128],
                                  v_sb[:, tau + c, kvh, :], probsT[:, c, :],
                                  start=(c == 0), stop=(c == WIN - 1)), waits=w)
            enc_l.append(V(lambda v, qh=qh, tau=tau:
                           v.tensor_copy(encT[:, qh, tau % 2, :],
                                         ps_b[:, 0:128]),
                           waits=[("T", vav)],
                           rd=[ps_b[:, 0:128]], wr=[encT[:, qh, tau % 2, :]]))
            if it == 7 and tau % 2 == 1:
                emit_outproj(tau)

        def emit_outproj(tau):
            nonlocal op_guard
            if True:
                v_enc = enc_l[-1]
                opmm = []
                opev = []
                sld = {}
                if PS == 1:
                    for tpar in range(2):
                        tt = tau - 1 + tpar
                        sld[tpar] = p.dma("sync", s_stage[:, tpar, :],
                                          s_dram[tt * 128:(tt + 1) * 128, :],
                                          waits=[st_["s_ld"][tt % 2]] if (tt % 2) in st_["s_ld"] else ())
                for dch in range(4):
                    for nh in range(8):
                        n_global = PS * 8 + nh
                        wbuf2 = nh % 2
                        wv = [("T", opmm[-3])] if len(opmm) >= 3 else (bar("T") + v_pc_wo)
                        vwo = p.dma("sync", wo_sb[:, wbuf2, :],
                                    wo_bf[n_global][:, dch * 512:(dch + 1) * 512],
                                    waits=wv)
                        for tpar in range(2):
                            pbo = ps_a if tpar == 0 else ps_b
                            w = [vwo]
                            if nh == 0:
                                w.append(("V", v_enc))
                                w.append(("S", ve2_l[-1]))
                                if opev:
                                    w.append(opev[-2 + tpar] if len(opev) >= 2 else opev[-1])
                            vo = T(lambda t_, nh=nh, tpar=tpar, pbo=pbo, wbuf2=wbuf2:
                                   t_.matmul(pbo[:, 0:512], encT[:, nh, tpar, :],
                                             wo_sb[:, wbuf2, :], start=(nh == 0),
                                             stop=(nh == 7)), waits=w)
                            opmm.append(vo)
                            if nh == 7:
                                if PS == 0:
                                    ve = ("S", S(lambda s, tpar=tpar, dch=dch, pbo=pbo:
                                                 s.copy(out=s_stage[:, tpar, dch * 512:(dch + 1) * 512],
                                                        in_=pbo[:, 0:512]),
                                                 waits=[("T", vo)],
                                                 rd=[pbo[:, 0:512]],
                                                 wr=[s_stage[:, tpar, dch * 512:(dch + 1) * 512]]))
                                else:
                                    ve = ("V", V(lambda v, tpar=tpar, dch=dch, pbo=pbo:
                                                 v.tensor_tensor(s_stage[:, tpar, dch * 512:(dch + 1) * 512],
                                                                 pbo[:, 0:512],
                                                                 s_stage[:, tpar, dch * 512:(dch + 1) * 512],
                                                                 ALU.add),
                                                 waits=[("T", vo), sld[tpar]],
                                                 rd=[pbo[:, 0:512],
                                                     s_stage[:, tpar, dch * 512:(dch + 1) * 512]],
                                                 wr=[s_stage[:, tpar, dch * 512:(dch + 1) * 512]]))
                                opev.append(ve)
                for tpar in range(2):
                    tt = tau - 1 + tpar
                    vsd = p.dma("scalar", s_dram[tt * 128:(tt + 1) * 128, :],
                                s_stage[:, tpar, :], waits=[opev[6 + tpar]])
                    st_["s_ld"][tt % 2] = vsd
                op_guard = [opev[-1], opev[-2]]

        emit_AT(0)
        emit_AS(0)
        emit_AT(1)
        for gi in range(64):
            emit_B1a(gi)
            if gi + 1 < 64:
                emit_AS(gi + 1)
            emit_B1b(gi)
            if gi + 2 < 64:
                emit_AT(gi + 2)
            emit_B2(gi)
        return p.qbar("scalar")

    def _rms_rope(nh, rope, ti, scale, is_q, ab=0):
        n = nh * 128
        V(lambda v: v.tensor_tensor(rms_t[:, 0:n], rows_a[:, 0:n], rows_a[:, 0:n],
                                    ALU.mult), waits=v_consts,
          rd=[rows_a[:, 0:n]], wr=[rms_t[:, 0:n]])
        vv = V(lambda v: v.tensor_reduce(rms_s[:, 0:nh],
                                         rms_t.rearrange("p (a h) -> p a h", h=128)[:, 0:nh, :],
                                         AXX, ALU.add),
               rd=[rms_t[:, 0:n]], wr=[rms_s[:, 0:nh]])
        S(lambda s: s.activation(rms_s[:, 8:8 + nh], rms_s[:, 0:nh], ACT.Ln,
                                 scale=actc_sb[:, 1:2], bias=actc_sb[:, 0:1]),
          waits=[("V", vv)] + v_consts,
          rd=[rms_s[:, 0:nh], actc_sb[:, 0:2]], wr=[rms_s[:, 8:8 + nh]])
        if is_q:
            vs = S(lambda s: s.activation(rms_s[:, 8:8 + nh], rms_s[:, 8:8 + nh],
                                          ACT.Exp, scale=actc_sb[:, 2:3], bias=actc_sb[:, 3:4]),
                   rd=[rms_s[:, 8:8 + nh], actc_sb[:, 2:4]], wr=[rms_s[:, 8:8 + nh]])
        else:
            vs = S(lambda s: s.activation(rms_s[:, 8:8 + nh], rms_s[:, 8:8 + nh],
                                          ACT.Exp, scale=actc_sb[:, 2:3]),
                   rd=[rms_s[:, 8:8 + nh], actc_sb[:, 2:3]], wr=[rms_s[:, 8:8 + nh]])
        ra = rows_a.rearrange("p (a h) -> p a h", h=128)
        rb = rows_b.rearrange("p (a h) -> p a h", h=128)
        rt = rms_t.rearrange("p (a h) -> p a h", h=128)
        V(lambda v: v.tensor_tensor(ra[:, 0:nh, :], ra[:, 0:nh, :],
                                    rms_s[:, 8:8 + nh].unsqueeze(2).broadcast_to([128, nh, 128]),
                                    ALU.mult), waits=[("S", vs)],
          rd=[rows_a[:, 0:n], rms_s[:, 8:8 + nh]], wr=[rows_a[:, 0:n]])
        V(lambda v: v.tensor_tensor(ra[:, 0:nh, :], ra[:, 0:nh, :],
                                    scale[:, :].unsqueeze(1).broadcast_to([128, nh, 128]),
                                    ALU.mult),
          rd=[rows_a[:, 0:n], scale[:, :]], wr=[rows_a[:, 0:n]])
        sin = rope[:, 0, ti, :].unsqueeze(1).broadcast_to([128, nh, 64])
        cos = rope[:, 1, ti, :].unsqueeze(1).broadcast_to([128, nh, 64])
        rdrope = [rope[:, 0, ti, :], rope[:, 1, ti, :]]
        rbs = rows_b[:, ab * 128:ab * 128 + n]
        V(lambda v: v.tensor_tensor(rb[:, ab:ab + nh, 0:64], ra[:, 0:nh, 0:64], cos, ALU.mult),
          rd=[rows_a[:, 0:n]] + rdrope, wr=[rbs])
        V(lambda v: v.tensor_tensor(rt[:, 0:nh, 0:64], ra[:, 0:nh, 64:128], sin, ALU.mult),
          rd=[rows_a[:, 0:n]] + rdrope, wr=[rms_t[:, 0:n]])
        V(lambda v: v.tensor_tensor(rb[:, ab:ab + nh, 0:64], rb[:, ab:ab + nh, 0:64],
                                    rt[:, 0:nh, 0:64], ALU.subtract),
          rd=[rbs, rms_t[:, 0:n]], wr=[rbs])
        V(lambda v: v.tensor_tensor(rb[:, ab:ab + nh, 64:128], ra[:, 0:nh, 64:128], cos, ALU.mult),
          rd=[rows_a[:, 0:n]] + rdrope, wr=[rbs])
        V(lambda v: v.tensor_tensor(rt[:, 0:nh, 0:64], ra[:, 0:nh, 0:64], sin, ALU.mult),
          rd=[rows_a[:, 0:n]] + rdrope, wr=[rms_t[:, 0:n]])
        V(lambda v: v.tensor_tensor(rb[:, ab:ab + nh, 64:128], rb[:, ab:ab + nh, 64:128],
                                    rt[:, 0:nh, 0:64], ALU.add),
          rd=[rbs, rms_t[:, 0:n]], wr=[rbs])

    attn_pass(0)

    vw = p.dma("sync", sc_all[:, :, :], scores_all.rearrange("(t p) e -> p t e", p=128),
               waits=[("CC", v_cc_sc)])
    SCA, SELT, SELR, POST = sc_all[:, :, :], sel_t[:, :, :], selr[:, :, :], pos_t[:, :, :]
    RA, RB, RC, RD = r_a[:, :, :], r_b[:, :, :], r_c[:, :, :], r_d[:, :, :]
    RK, SG = rank_t[:, :, :], selg[:, :, :]
    V(lambda v: v.tensor_tensor(sc_all[:, :, :], sc_all[:, :, :],
                                bias_t[:, :].unsqueeze(1).broadcast_to([128, 64, 32]), ALU.add),
      waits=[vw, ("V", v_bias)], rd=[SCA, bias_t[:, :]], wr=[SCA])
    g4 = sc_all.rearrange("p t (g f) -> p t g f", f=4)
    V(lambda v: v.tensor_tensor(r_a[:, :, :], g4[:, :, :, 0], g4[:, :, :, 1], ALU.max),
      rd=[SCA], wr=[RA])
    V(lambda v: v.tensor_tensor(r_b[:, :, :], g4[:, :, :, 0], g4[:, :, :, 1], ALU.min),
      rd=[SCA], wr=[RB])
    V(lambda v: v.tensor_tensor(r_c[:, :, :], g4[:, :, :, 2], g4[:, :, :, 3], ALU.max),
      rd=[SCA], wr=[RC])
    V(lambda v: v.tensor_tensor(r_d[:, :, :], g4[:, :, :, 2], g4[:, :, :, 3], ALU.min),
      rd=[SCA], wr=[RD])
    V(lambda v: v.tensor_tensor(selg[:, :, :], r_a[:, :, :], r_c[:, :, :], ALU.is_ge),
      rd=[RA, RC], wr=[SG])
    V(lambda v: v.tensor_tensor(rank_t[:, :, :], r_b[:, :, :], r_d[:, :, :], ALU.subtract),
      rd=[RB, RD], wr=[RK])
    V(lambda v: v.tensor_tensor(rank_t[:, :, :], rank_t[:, :, :], selg[:, :, :], ALU.mult),
      rd=[RK, SG], wr=[RK])
    V(lambda v: v.tensor_tensor(rank_t[:, :, :], rank_t[:, :, :], r_d[:, :, :], ALU.add),
      rd=[RK, RD], wr=[RK])
    V(lambda v: v.tensor_tensor(r_b[:, :, :], r_a[:, :, :], r_c[:, :, :], ALU.min),
      rd=[RA, RC], wr=[RB])
    V(lambda v: v.tensor_tensor(rank_t[:, :, :], rank_t[:, :, :], r_b[:, :, :], ALU.max),
      rd=[RK, RB], wr=[RK])
    V(lambda v: v.tensor_tensor(r_a[:, :, :], r_a[:, :, :], r_c[:, :, :], ALU.max),
      rd=[RA, RC], wr=[RA])
    V(lambda v: v.tensor_tensor(r_a[:, :, :], r_a[:, :, :], rank_t[:, :, :], ALU.add),
      rd=[RA, RK], wr=[RA])
    V(lambda v: v.memset(rank_t[:, :, :], 0.0), rd=[], wr=[RK])
    for dd in range(1, 8):
        n = 8 - dd
        V(lambda v, dd=dd, n=n: v.tensor_tensor(r_b[:, :, 0:n], r_a[:, :, dd:8],
                                                r_a[:, :, 0:n], ALU.is_gt),
          rd=[RA], wr=[RB])
        V(lambda v, dd=dd, n=n: v.tensor_tensor(rank_t[:, :, 0:n], rank_t[:, :, 0:n],
                                                r_b[:, :, 0:n], ALU.add),
          rd=[RK, RB], wr=[RK])
        V(lambda v, dd=dd, n=n: v.tensor_tensor(r_b[:, :, dd:8], r_a[:, :, 0:n],
                                                r_a[:, :, dd:8], ALU.is_ge),
          rd=[RA], wr=[RB])
        V(lambda v, dd=dd, n=n: v.tensor_tensor(rank_t[:, :, dd:8], rank_t[:, :, dd:8],
                                                r_b[:, :, dd:8], ALU.add),
          rd=[RK, RB], wr=[RK])
    V(lambda v: v.tensor_scalar(selg[:, :, :], rank_t[:, :, :], 3.5, None, ALU.is_lt),
      rd=[RK], wr=[SG])
    V(lambda v: v.tensor_tensor(sc_all.rearrange("p t (g f) -> p t g f", f=4), g4,
                                selg.unsqueeze(3).broadcast_to([128, 64, 8, 4]), ALU.mult),
      rd=[SCA, SG], wr=[SCA])
    V(lambda v: v.tensor_scalar(selr[:, :, :], sc_all[:, :, :], 0.0, None, ALU.is_equal),
      rd=[SCA], wr=[SELR])
    V(lambda v: v.tensor_tensor(selr[:, :, :], selr[:, :, :],
                                epse_t[:, :].unsqueeze(1).broadcast_to([128, 64, 32]), ALU.mult),
      rd=[SELR, epse_t[:, :]], wr=[SELR])
    V(lambda v: v.tensor_tensor(sc_all[:, :, :], sc_all[:, :, :], selr[:, :, :],
                                ALU.subtract), rd=[SCA, SELR], wr=[SCA])
    V(lambda v: v.memset(sel_t[:, :, :], 0.0), rd=[], wr=[SELT])
    for r in range(4):
        V(lambda v: v.tensor_reduce(m_max[:, :], sc_all[:, :, :], AXX, ALU.max),
          rd=[SCA], wr=[m_max[:, :]])
        V(lambda v: v.tensor_tensor(selr[:, :, :], sc_all[:, :, :],
                                    m_max[:, :].unsqueeze(2).broadcast_to([128, 64, 32]),
                                    ALU.is_ge),
          rd=[SCA, m_max[:, :]], wr=[SELR])
        V(lambda v: v.tensor_tensor(sel_t[:, :, :], sel_t[:, :, :], selr[:, :, :], ALU.add),
          rd=[SELT, SELR], wr=[SELT])
        V(lambda v: v.scalar_tensor_tensor(sc_all[:, :, :], selr[:, :, :], -1e30,
                                           sc_all[:, :, :], ALU.mult, ALU.add),
          rd=[SCA, SELR], wr=[SCA])
    vw2 = p.dma("sync", sc_all[:, :, :], scores_all.rearrange("(t p) e -> p t e", p=128),
                waits=bar("V"))
    V(lambda v: v.tensor_tensor(sc_all[:, :, :], sc_all[:, :, :], sel_t[:, :, :],
                                ALU.mult), waits=[vw2], rd=[SCA, SELT], wr=[SCA])
    V(lambda v: v.tensor_reduce(wsum[:, :], sc_all[:, :, :], AXX, ALU.add),
      rd=[SCA], wr=[wsum[:, :]])
    V(lambda v: v.tensor_scalar(wsum[:, :], wsum[:, :], 1e-20, None, ALU.add),
      rd=[wsum[:, :]], wr=[wsum[:, :]])
    V(lambda v: v.reciprocal(wsum[:, :], wsum[:, :]),
      rd=[wsum[:, :]], wr=[wsum[:, :]])
    V(lambda v: v.tensor_tensor(sc_all[:, :, :], sc_all[:, :, :],
                                wsum[:, :].unsqueeze(2).broadcast_to([128, 64, 32]), ALU.mult),
      rd=[SCA, wsum[:, :]], wr=[SCA])
    V(lambda v: v.tensor_scalar(sc_all[:, :, :], sc_all[:, :, :], 2.5, None, ALU.mult),
      rd=[SCA], wr=[SCA])
    v_selbf = V(lambda v: v.tensor_copy(sel_bf[:, :, :], sel_t[:, :, :]),
                rd=[SELT], wr=[sel_bf[:, :, :]])
    cnt_ev = 0
    for j in range(4):
        w = bar("V", "S") + [("V", v_selbf)] if j == 0 else [("V", cnt_ev)]
        vt = T(lambda t_, j=j: t_.matmul(ps_a[0:1, 0:512], ones_cb_sb[:, :],
                                         sel_bf.rearrange("p t e -> p (t e)")[:, j * 512:(j + 1) * 512],
                                         start=True, stop=True), waits=w)
        cnt_ev = V(lambda v, j=j: v.tensor_copy(cnt_t.rearrange("p t e -> p (t e)")[0:1, j * 512:(j + 1) * 512],
                                                ps_a[0:1, 0:512]), waits=[("T", vt)],
                   rd=[ps_a[0:1, 0:512]],
                   wr=[cnt_t.rearrange("p t e -> p (t e)")[0:1, j * 512:(j + 1) * 512]])
    V(lambda v: v.memset(off_t[0:1, 0, :], 0.0), waits=[("V", cnt_ev)],
      rd=[], wr=[off_t[0:1, 0, :]])
    for t in range(1, 64):
        V(lambda v, t=t: v.tensor_tensor(off_t[0:1, t, :], off_t[0:1, t - 1, :],
                                         cnt_t[0:1, t - 1, :], ALU.add),
          rd=[off_t[0:1, t - 1, :], cnt_t[0:1, t - 1, :]], wr=[off_t[0:1, t, :]])
    v_off = p.val("V")
    pos_ev = []
    for t in range(64):
        pbp = ps_b if t % 2 == 0 else ps_a
        w = ([("V", v_off)] + v_consts) if t == 0 else []
        if t >= 2:
            w.append(("V", pos_ev[t - 2]))
        T(lambda t_, t=t, pbp=pbp: t_.matmul(pbp[:, 0:32], tri_cum_sb[:, :],
                                             sel_bf[:, t, :], start=True, stop=False),
          waits=w)
        vp = T(lambda t_, t=t, pbp=pbp: t_.matmul(pbp[:, 0:32], ones_f_sb[:1, :],
                                                  off_t[0:1, t, :], start=False, stop=True))
        pos_ev.append(V(lambda v, t=t, pbp=pbp: v.tensor_copy(pos_t[:, t, :],
                                                              pbp[:, 0:32]),
                        waits=[("T", vp)],
                        rd=[pbp[:, 0:32]], wr=[pos_t[:, t, :]]))
    V(lambda v: v.tensor_scalar(selr[:, :, :], pos_t[:, :, :], float(CAP), None, ALU.is_lt),
      rd=[POST], wr=[SELR])
    V(lambda v: v.tensor_tensor(selr[:, :, :], selr[:, :, :], sel_t[:, :, :], ALU.mult),
      rd=[SELR, SELT], wr=[SELR])
    V(lambda v: v.tensor_scalar(pos_t[:, :, :], pos_t[:, :, :], 1.0, None, ALU.add),
      rd=[POST], wr=[POST])
    V(lambda v: v.tensor_tensor(pos_t[:, :, :], pos_t[:, :, :], selr[:, :, :], ALU.mult),
      rd=[POST, SELR], wr=[POST])
    V(lambda v: v.tensor_scalar(pos_t[:, :, :], pos_t[:, :, :], 1.0, None, ALU.subtract),
      rd=[POST], wr=[POST])
    V(lambda v: v.tensor_copy(sel_bf[:, :, :], sc_all[:, :, :]),
      rd=[SCA], wr=[sel_bf[:, :, :]])
    V(lambda v: v.tensor_tensor(sel_t[:, :, :], sc_all[:, :, :], sel_bf[:, :, :],
                                ALU.subtract),
      rd=[SCA, sel_bf[:, :, :]], wr=[SELT])
    for e in range(4):
        eb = esel_t[:, e, :].unsqueeze(1).broadcast_to([128, 64, 32])
        ebr = esel_t[:, e, :]
        V(lambda v, eb=eb: v.tensor_tensor(selr[:, :, :], pos_t[:, :, :], eb, ALU.mult),
          rd=[POST, ebr], wr=[SELR])
        V(lambda v, e=e: v.tensor_reduce(own3[:, e, :], selr[:, :, :], AXX, ALU.add),
          rd=[SELR], wr=[own3[:, e, :]])
        V(lambda v, eb=eb: v.tensor_tensor(selr[:, :, :], sel_bf[:, :, :], eb, ALU.mult),
          rd=[sel_bf[:, :, :], ebr], wr=[SELR])
        V(lambda v, e=e: v.tensor_reduce(own3[:, 4 + e, :], selr[:, :, :], AXX, ALU.add),
          rd=[SELR], wr=[own3[:, 4 + e, :]])
        V(lambda v, eb=eb: v.tensor_tensor(selr[:, :, :], sel_t[:, :, :], eb, ALU.mult),
          rd=[SELT, ebr], wr=[SELR])
        V(lambda v, e=e: v.tensor_reduce(own3[:, 8 + e, :], selr[:, :, :], AXX, ALU.add),
          rd=[SELR], wr=[own3[:, 8 + e, :]])
    V(lambda v: v.tensor_copy(own_i[:, :, :], own3[:, 0:4, :]),
      rd=[own3[:, 0:4, :]], wr=[own_i[:, :, :]])
    V(lambda v: v.tensor_copy(own_hi[:, :, :], own3[:, 4:8, :]),
      rd=[own3[:, 4:8, :]], wr=[own_hi[:, :, :]])
    v_own = V(lambda v: v.tensor_copy(own_lo[:, :, :], own3[:, 8:12, :]),
              rd=[own3[:, 8:12, :]], wr=[own_lo[:, :, :]])
    d1 = p.dma("scalar", st_dump[:, :, :], own_i[:, :, :], waits=[("V", v_own)])
    d2 = p.dma("scalar", whi_dump[:, :, :], own_hi[:, :, :])
    d3 = p.dma("scalar", wlo_dump[:, :, :], own_lo[:, :, :])
    for ee in range(4):
        rb1 = p.dma("sync", ls_idx[ee * 32:(ee + 1) * 32, :].rearrange("g (pm t) -> g pm t", pm=4),
                    st_dump.rearrange("(g pm) e t -> e g pm t", pm=4)[ee],
                    waits=[d1] if ee == 0 else ())
        rb2 = p.dma("sync", ls_whi[ee * 32:(ee + 1) * 32, :].rearrange("g (pm t) -> g pm t", pm=4),
                    whi_dump.rearrange("(g pm) e t -> e g pm t", pm=4)[ee],
                    waits=[d2] if ee == 0 else ())
        rb3 = p.dma("sync", ls_wlo[ee * 32:(ee + 1) * 32, :].rearrange("g (pm t) -> g pm t", pm=4),
                    wlo_dump.rearrange("(g pm) e t -> e g pm t", pm=4)[ee],
                    waits=[d3] if ee == 0 else ())
    for gq in range(4):
        G(lambda g, gq=gq: g.iota(ls_tok[gq * 32:(gq + 1) * 32, :],
                                  pattern=[[1, 4], [128, 64]],
                                  base=1, channel_multiplier=4),
          rd=[], wr=[ls_tok[gq * 32:(gq + 1) * 32, :]])
    v_iota = p.val("G")
    vls1 = G(lambda g: g.local_scatter(lso_tok[:, :], ls_tok[:, :], ls_idx[:, :],
                                       128, CAP, 256), waits=[rb1, ("G", v_iota)],
             rd=[ls_tok[:, :], ls_idx[:, :]], wr=[lso_tok[:, :]])
    vls2 = G(lambda g: g.local_scatter(lso_whi[:, :], ls_whi[:, :], ls_idx[:, :],
                                       128, CAP, 256), waits=[rb2],
             rd=[ls_whi[:, :], ls_idx[:, :]], wr=[lso_whi[:, :]])
    vls3 = G(lambda g: g.local_scatter(lso_wlo[:, :], ls_wlo[:, :], ls_idx[:, :],
                                       128, CAP, 256), waits=[rb3],
             rd=[ls_wlo[:, :], ls_idx[:, :]], wr=[lso_wlo[:, :]])
    vcf = V(lambda v: v.tensor_copy(lso_tok_f[:, :], lso_tok[:, :]), waits=[("G", vls1)],
            rd=[lso_tok[:, :]], wr=[lso_tok_f[:, :]])
    mg_ev = []
    for (src_sb, dst0, wg, add) in [(lso_tok_f, 0, [("V", vcf)], False),
                                    (lso_whi, CAP, [("G", vls2)], False),
                                    (lso_wlo, CAP, [("G", vls3)], True)]:
        lhs = i4f_sb if src_sb is lso_tok_f else i4b_sb
        for j in range(2):
            w = (wg + v_consts + (bar("V") if not mg_ev else [("V", mg_ev[-1])])) if j == 0 else \
                [("V", mg_ev[-1])]
            vt = T(lambda t_, j=j, src_sb=src_sb, lhs=lhs:
                   t_.matmul(ps_b[0:4, 0:512], lhs[:, :],
                             src_sb[:, j * 512:(j + 1) * 512], start=True, stop=True),
                   waits=w)
            if add:
                mg_ev.append(V(lambda v, j=j, dst0=dst0:
                               v.tensor_add(mrg[:, dst0 + j * 512:dst0 + (j + 1) * 512],
                                            mrg[:, dst0 + j * 512:dst0 + (j + 1) * 512],
                                            ps_b[0:4, 0:512]), waits=[("T", vt)],
                               rd=[mrg[:, dst0 + j * 512:dst0 + (j + 1) * 512],
                                   ps_b[0:4, 0:512]],
                               wr=[mrg[:, dst0 + j * 512:dst0 + (j + 1) * 512]]))
            else:
                mg_ev.append(V(lambda v, j=j, dst0=dst0:
                               v.tensor_copy(mrg[:, dst0 + j * 512:dst0 + (j + 1) * 512],
                                             ps_b[0:4, 0:512]), waits=[("T", vt)],
                               rd=[ps_b[0:4, 0:512]],
                               wr=[mrg[:, dst0 + j * 512:dst0 + (j + 1) * 512]]))
    V(lambda v: v.tensor_scalar(mrg[:, 2 * CAP:3 * CAP], mrg[:, 0:CAP], 0.0, None,
                                ALU.is_equal), waits=[("V", mg_ev[-1])],
      rd=[mrg[:, 0:CAP]], wr=[mrg[:, 2 * CAP:3 * CAP]])
    V(lambda v: v.tensor_scalar(mrg[:, 2 * CAP:3 * CAP], mrg[:, 2 * CAP:3 * CAP],
                                8193.0, None, ALU.mult),
      rd=[mrg[:, 2 * CAP:3 * CAP]], wr=[mrg[:, 2 * CAP:3 * CAP]])
    V(lambda v: v.tensor_scalar(mrg[:, 0:CAP], mrg[:, 0:CAP], 1.0, None, ALU.subtract),
      rd=[mrg[:, 0:CAP]], wr=[mrg[:, 0:CAP]])
    V(lambda v: v.tensor_scalar(gidx_i[:, :].bitcast(I16) if False else gidx_i[:, :],
                                mrg[:, 0:CAP], 0.0, None, ALU.max),
      rd=[mrg[:, 0:CAP]], wr=[gidx_i[:, :]])
    V(lambda v: v.tensor_add(mrg[:, 2 * CAP:3 * CAP], mrg[:, 2 * CAP:3 * CAP],
                             mrg[:, 0:CAP]),
      rd=[mrg[:, 2 * CAP:3 * CAP], mrg[:, 0:CAP]], wr=[mrg[:, 2 * CAP:3 * CAP]])
    v_sidx = V(lambda v: v.tensor_copy(sidx_i[:, :], mrg[:, 2 * CAP:3 * CAP]),
               rd=[mrg[:, 2 * CAP:3 * CAP]], wr=[sidx_i[:, :]])
    wt_ev = v_sidx
    for stt in range(8):
        vt = T(lambda t_, stt=stt:
               t_.transpose(ps_a[:, 0:4],
                            mrg[0:4, CAP + stt * 128:CAP + (stt + 1) * 128],
                            identf_sb[0:4, 0:4]),
               waits=[("V", wt_ev)] + (v_consts if stt == 0 else []))
        wt_ev = V(lambda v, stt=stt:
                  v.tensor_copy(wslot[:, :, stt], ps_a[:, 0:4]),
                  waits=[("T", vt)],
                  rd=[ps_a[:, 0:4]], wr=[wslot[:, :, stt]])
    v_wslot = wt_ev
    gidx_st = rbi[0:4, 0:1024]
    sidx_st = rbi[0:4, 1024:2048]
    V(lambda v: v.tensor_copy(gidx_st.rearrange("e (s j) -> e s j", s=16),
                              gidx_i.rearrange("e (j s) -> e s j", s=16)),
      waits=[("V", v_wslot)],
      rd=[gidx_i[:, :]], wr=[gidx_st])
    v_st2 = V(lambda v: v.tensor_copy(sidx_st.rearrange("e (s j) -> e s j", s=16),
                                      sidx_i.rearrange("e (j s) -> e s j", s=16)),
              rd=[sidx_i[:, :]], wr=[sidx_st])
    p.dma("scalar", gidx_dump[:, :], gidx_st, waits=[("V", v_st2)])
    p.dma("scalar", sidx_dump[:, :], sidx_st)
    v_idx_dumps = p.qbar("scalar")
    ln_dguard[0] = list(v_idx_dumps)

    if debug:
        p.dma("sync", dbg["gidx_o"][:, :], gidx_dump[:, :], waits=v_idx_dumps)
        p.dma("sync", dbg["sidx_o"][:, :], sidx_dump[:, :])
        p.dma("scalar", dbg["wslot_o"][:, :, :], wslot[:, :, :],
              waits=[("V", v_wslot)])

    if stage < 3:
        _finish(p, nc)
        return nc, dbg

    if stage >= 4:
        attn_pass(1)
    v_s_all = p.qbar("scalar")

    if debug:
        for tt in range(8):
            p.dma("sync", dbg["s_o"][tt * 128:(tt + 1) * 128, :],
                  s_dram[tt * 128:(tt + 1) * 128, :], waits=v_s_all)

    if stage < 5:
        _finish(p, nc)
        return nc, dbg

    wph = bar("V", "S", "T")
    v_scat = None
    v_gath = None
    prev_up_last = 0
    prev_dn_last = 0
    pending_scat = None
    pbg = {0: 0, 1: 0}
    for e in range(4):
        src_g = gidx_dump[e:e + 1, :].rearrange("o (s j) -> (o s) j", s=16)
        src_s = sidx_dump[e:e + 1, :].rearrange("o (s j) -> (o s) j", s=16)
        w_rb = list(v_idx_dumps)
        if e > 0:
            w_rb.append(v_gath)
        for gq in range(8):
            r2 = p.dma("sync", gidx_rep[16 * gq:16 * gq + 16, :], src_g,
                       waits=w_rb if gq == 0 else ())
        wg = [r2, ("CC", v_cc_xn)] + (wph if e == 0 else [])
        if e > 0:
            wg.append(("T", prev_up_last))
        for hfg in range(2):
            v_gath = Gd(lambda g, hfg=hfg: g.dma_gather(
                xeT2[hfg][:, :, :], xn_all,
                gidx_rep[:, hfg * 32:(hfg + 1) * 32], 512, 512, D,
                transpose=True, queue_num=1), waits=wg if hfg == 0 else ())
        up_src = up_wT.rearrange("e (c p) n -> e p c n", p=128)
        up_mm = []
        v_hT = 0
        for j in range(8):
            jb = j % 4
            w = (wph + v_idx_dumps) if (e == 0 and j == 0) else []
            if j >= 4:
                w = [("T", up_mm[2 * (j - 4) + 1])]
            elif e > 0 and j < 4:
                w = [("T", prev_up_last)]
            vwu = p.dma("gpsimd", wst[:, :, jb * 128:(jb + 1) * 128],
                        up_src[e][:, :, j * 128:(j + 1) * 128], waits=w)
            if j == 0:
                if pending_scat is not None:
                    wsc = [("V", pending_scat), r4] + v_pzero
                    if v_scat:
                        wsc.append(v_scat)
                    for hfg in range(2):
                        v_scat = Gd(lambda g, hfg=hfg, ep=e - 1: g.dma_scatter_add(
                            partial, yew[:, hfg * 4:(hfg + 1) * 4, :],
                            sidx_rep[:, hfg * 32:(hfg + 1) * 32], 512, 512, D,
                            queue_num=1), waits=wsc if hfg == 0 else ())
                    pending_scat = None
                for gq in range(8):
                    r4 = p.dma("sync", sidx_rep[16 * gq:16 * gq + 16, :], src_s,
                               waits=([v_scat] if v_scat else ()) if gq == 0 else ())
            for half in range(2):
                pbid = (j * 2 + half) % 2
                pb = ps_l1 if pbid == 0 else ps_l2
                for c in range(16):
                    w = [vwu, v_gath, ("V", pbg[pbid])] if c == 0 else []
                    vm = T(lambda t_, c=c, half=half, jb=jb, pb=pb:
                           t_.matmul(pb[:, 0:512], wst[:, c, jb * 128:(jb + 1) * 128],
                                     xeT2[half][:, c, :],
                                     start=(c == 0), stop=(c == 15)), waits=w)
                up_mm.append(vm)
                rlu = rows_a[:, pbid * 512:(pbid + 1) * 512]
                V(lambda v, pb=pb, rlu=rlu:
                  v.tensor_scalar(rlu, pb[:, 0:512], 0.0, None, ALU.max),
                  waits=[("T", vm)],
                  rd=[pb[:, 0:512]], wr=[rlu])
                v_hT = V(lambda v, j=j, half=half, pb=pb, rlu=rlu:
                         v.tensor_tensor(hT[:, j, half * 512:(half + 1) * 512],
                                         rlu, pb[:, 0:512], ALU.mult),
                         rd=[rlu, pb[:, 0:512]],
                         wr=[hT[:, j, half * 512:(half + 1) * 512]])
                pbg[pbid] = v_hT
        prev_up_last = up_mm[-1]
        dn_src = down_wT.rearrange("e (j p) n -> e p j n", p=128)
        yew_last = 0
        for dch in range(4):
            db = dch % 2
            w = [("V", v_hT)] if dch == 0 else [("T", prev_dn_last)]
            if e > 0 and dch == 0:
                w.append(("T", prev_dn_last))
            vwd = p.dma("gpsimd", dw_sb[:, db, :, :],
                        dn_src[e][:, :, dch * 512:(dch + 1) * 512], waits=w)
            for gst in range(8):
                pbid = gst % 2
                pb = ps_l1 if pbid == 0 else ps_l2
                for j in range(8):
                    w = [vwd, ("V", v_hT), ("V", pbg[pbid])] if j == 0 else []
                    vm = T(lambda t_, gst=gst, j=j, db=db, pb=pb:
                           t_.matmul(pb[:, 0:512], hT[:, j, gst * 128:(gst + 1) * 128],
                                     dw_sb[:, db, j, :], start=(j == 0), stop=(j == 7)),
                           waits=w)
                prev_dn_last = vm
                ve = V(lambda v, gst=gst, dch=dch, e=e, pb=pb:
                       v.tensor_scalar(yew[:, gst, dch * 512:(dch + 1) * 512],
                                       pb[:, 0:512], wslot[:, e, gst:gst + 1], None,
                                       ALU.mult), waits=[("T", vm)] + ([v_scat] if v_scat else []),
                       rd=[pb[:, 0:512], wslot[:, e, gst:gst + 1]],
                       wr=[yew[:, gst, dch * 512:(dch + 1) * 512]])
                pbg[pbid] = ve
                yew_last = ve
        pending_scat = yew_last

    wsc = [("V", pending_scat), r4] + v_pzero
    if v_scat:
        wsc.append(v_scat)
    for hfg in range(2):
        v_scat = Gd(lambda g, hfg=hfg: g.dma_scatter_add(
            partial, yew[:, hfg * 4:(hfg + 1) * 4, :],
            sidx_rep[:, hfg * 32:(hfg + 1) * 32], 512, 512, D,
            queue_num=1), waits=wsc if hfg == 0 else ())

    CCe(lambda g: g.collective_compute("ReduceScatter", ALU.add, RG,
                                       [partial[0:8192, :]], [routed]),
        waits=[v_scat])
    v_cc_rs = p.val("CC")

    if debug:
        p.dma("sync", dbg["routed_o"][:, :], routed[:, :], waits=[("CC", v_cc_rs)])

    if stage < 6:
        _finish(p, nc)
        return nc, dbg

    wph5 = bar("V", "S", "T")
    up_s = shup_bf.rearrange("(c p) n -> p c n", p=128)
    dn_s = shdn_bf.rearrange("(c p) n -> p c n", p=128)
    up5 = []
    pbg5 = {0: 0, 1: 0}
    cmb_last = 0
    out_tok = {0: None, 1: None}
    shdump = None
    for slab in range(4):
        h1_ev = 0
        pbg5[0] = max(pbg5[0], cmb_last)
        pbg5[1] = max(pbg5[1], cmb_last)
        for j in range(32):
            jb = j % 4
            w = [("T", up5[-4])] if len(up5) >= 4 else []
            if slab == 0 and j < 4:
                w = w + wph5 + v_pc_shup
            vwu = p.dma("sync" if j % 2 == 0 else "scalar",
                        shw_up[:, jb, :, :], up_s[:, :, j * 128:(j + 1) * 128],
                        waits=w)
            pbid = j % 2
            pb = ps_a if pbid == 0 else ps_b
            for c in range(16):
                w = [vwu, ("V", pbg5[pbid])] if c == 0 else []
                vm = T(lambda t_, j=j, c=c, jb=jb, slab=slab, pb=pb:
                       t_.matmul(pb[:, 0:256], shw_up[:, jb, c, :],
                                 xnT[:, c, slab * 256:(slab + 1) * 256],
                                 start=(c == 0), stop=(c == 15)), waits=w)
            up5.append(vm)
            rlu = rows_a[:, pbid * 512:pbid * 512 + 256]
            V(lambda v, pb=pb, rlu=rlu:
              v.tensor_scalar(rlu, pb[:, 0:256], 0.0, None, ALU.max),
              waits=[("T", vm)],
              rd=[pb[:, 0:256]], wr=[rlu])
            h1_ev = V(lambda v, j=j, pb=pb, rlu=rlu:
                      v.tensor_tensor(h1T[:, j, :], rlu, pb[:, 0:256], ALU.mult),
                      rd=[rlu, pb[:, 0:256]], wr=[h1T[:, j, :]])
            pbg5[pbid] = h1_ev
        PBS = [ps_l1[:, 0:512], ps_l1[:, 512:1024], ps_l1[:, 1024:1536], ps_a[:, 0:512],
               ps_l2[:, 0:512], ps_l2[:, 512:1024], ps_l2[:, 1024:1536], ps_b[:, 0:512]]
        if slab == 0:
            dnmm_l = [0, 0]
        for j in range(32):
            jb = j % 2
            w = [("V", h1_ev), ("T", dnmm_l[jb])]
            if slab == 0 and j < 2:
                w += v_pc_shdn + [("CC", v_cc_rs)]
            vwd = p.dma("sync" if j % 2 == 0 else "scalar",
                        shw_dn[:, jb, :], dn_s[:, j, :], waits=w)
            for ts in range(2):
                for dch in range(4):
                    gidx5 = ts * 4 + dch
                    w2 = []
                    if gidx5 == 0:
                        w2 = [vwd, ("V", h1_ev)]
                    if j == 0:
                        w2.append(("V", cmb_last))
                    vm = T(lambda t_, j=j, ts=ts, dch=dch, jb=jb, pbv=PBS[gidx5]:
                           t_.matmul(pbv, h1T[:, j, ts * 128:(ts + 1) * 128],
                                     shw_dn[:, jb, dch * 512:(dch + 1) * 512],
                                     start=(j == 0), stop=(j == 31)), waits=w2)
            dnmm_l[jb] = vm
        v_dn = vm
        for ts in range(2):
            tt = slab * 2 + ts
            par = ts
            vrt = p.dma("sync", routed_sb[:, par, :], routed[tt * 128:(tt + 1) * 128, :],
                        waits=[("CC", v_cc_rs), ("V", cmb_last)])
            vsl = p.dma("sync", s_in[:, par, :], s_dram[tt * 128:(tt + 1) * 128, :],
                        waits=v_s_all + [("V", cmb_last)])
            vxo = p.dma("sync", xo_f32[:, par, :], x_own[tt * 128:(tt + 1) * 128, :],
                        waits=[("V", cmb_last)])
            if debug:
                p.dma("sync", dbg["sin_o"][tt * 128:(tt + 1) * 128, :],
                      s_in[:, par, :], waits=[vsl])
                p.dma("sync", dbg["rtd_o"][tt * 128:(tt + 1) * 128, :],
                      routed_sb[:, par, :], waits=[vrt])
            for dch in range(4):
                pbv = PBS[ts * 4 + dch]
                sl = slice(dch * 512, (dch + 1) * 512)
                if debug:
                    vsh = V(lambda v, pbv=pbv, sl=sl:
                            v.tensor_copy(ln_x[:, 0, sl], pbv),
                            waits=[("T", v_dn)] + ([shdump] if shdump else []),
                            rd=[pbv], wr=[ln_x[:, 0, sl]])
                    shdump = p.dma("sync", dbg["shared_o"][tt * 128:(tt + 1) * 128, sl],
                                   ln_x[:, 0, sl], waits=[("V", vsh)])
                vga = V(lambda v, pbv=pbv, par=par, sl=sl:
                        v.tensor_tensor(g_f32[:, par, sl], pbv, routed_sb[:, par, sl],
                                        ALU.add),
                        waits=[("T", v_dn), vrt] +
                        ([out_tok[par]] if out_tok[par] else []),
                        rd=[pbv, routed_sb[:, par, sl]], wr=[g_f32[:, par, sl]])
                vsg = S(lambda s, par=par, sl=sl:
                        s.activation(sg_f32[:, par, sl], g_f32[:, par, sl], ACT.Sigmoid),
                        waits=[("V", vga)],
                        rd=[g_f32[:, par, sl]], wr=[sg_f32[:, par, sl]])
                V(lambda v, par=par, sl=sl:
                  v.tensor_tensor(g_f32[:, par, sl], g_f32[:, par, sl],
                                  sg_f32[:, par, sl], ALU.mult), waits=[("S", vsg)],
                  rd=[g_f32[:, par, sl], sg_f32[:, par, sl]], wr=[g_f32[:, par, sl]])
                V(lambda v, par=par, sl=sl:
                  v.tensor_tensor(g_f32[:, par, sl], g_f32[:, par, sl], s_in[:, par, sl],
                                  ALU.mult), waits=[vsl],
                  rd=[g_f32[:, par, sl], s_in[:, par, sl]], wr=[g_f32[:, par, sl]])
                cmb_last = V(lambda v, par=par, sl=sl:
                             v.tensor_tensor(g_f32[:, par, sl], g_f32[:, par, sl],
                                             xo_f32[:, par, sl], ALU.add),
                             waits=[vxo],
                             rd=[g_f32[:, par, sl], xo_f32[:, par, sl]],
                             wr=[g_f32[:, par, sl]])
            out_tok[par] = p.dma("scalar", out[tt * 128:(tt + 1) * 128, :],
                                 g_f32[:, par, :], waits=[("V", cmb_last)])

    _finish(p, nc)
    nc._prog = p
    return nc, dbg


def _finish(p, nc):
    fin = [(s, v) for s, v in p.cnt.items() if v > 0]
    p.ops.append(("sync", lambda e: e.nop(), tuple(fin), None, 1, None, None))
    import contextlib
    with contextlib.ExitStack() as st:
        for name in p.cnt:
            p.semh[name] = st.enter_context(nc.semaphore(name))
        blk = st.enter_context(nc.Block())

        @blk.sync
        def _(eng):
            p.play("sync", eng)

        @blk.tensor
        def _(eng):
            p.play("tensor", eng)

        @blk.vector
        def _(eng):
            p.play("vector", eng)

        @blk.scalar
        def _(eng):
            p.play("scalar", eng)

        @blk.gpsimd
        def _(eng):
            p.play("gpsimd", eng)


import ml_dtypes

def make_core_inputs(inp, core):
    f32 = np.float32
    x = np.ascontiguousarray(np.asarray(inp["L0_attn_x"], f32).reshape(8192, D))
    chunk = core % 4
    g0 = core * 1024
    x_own = np.ascontiguousarray(x[g0:g0 + 1024])
    if chunk == 0:
        x_halo = np.zeros((1024, D), f32)
    else:
        x_halo = np.ascontiguousarray(x[g0 - 1024:g0])

    qk = np.asarray(inp["L0_attn_q_kernel"], f32)
    wq = np.ascontiguousarray(qk.transpose(1, 0, 2).reshape(D, 2048))
    kvk = np.asarray(inp["L0_attn_kv_kernel"], f32)
    wkv = np.ascontiguousarray(kvk.transpose(2, 1, 0, 3).reshape(D, 2048))
    wo = np.ascontiguousarray(np.asarray(inp["L0_attn_out_kernel"], f32))
    gate_wT = np.ascontiguousarray(np.asarray(inp["L0_ffn_gate_w"], f32).T)
    up = np.asarray(inp["L0_ffn_up_w"], f32)
    dn = np.asarray(inp["L0_ffn_down_w"], f32)
    up_wT = np.ascontiguousarray(up[4 * core:4 * core + 4].transpose(0, 2, 1))
    down_wT = np.ascontiguousarray(dn[4 * core:4 * core + 4].transpose(0, 2, 1))
    sh_upT = np.ascontiguousarray(np.asarray(inp["L0_ffn_shared_up_w"], f32).T)
    sh_dnT = np.ascontiguousarray(np.asarray(inp["L0_ffn_shared_down_w"], f32).T)
    qns = np.ascontiguousarray(np.tile(np.asarray(inp["L0_attn_q_norm_scale"], f32), (128, 1)).astype(ml_dtypes.bfloat16))
    kns = np.ascontiguousarray(np.tile(np.asarray(inp["L0_attn_k_norm_scale"], f32), (128, 1)).astype(ml_dtypes.bfloat16))

    brow = np.zeros((1, 256), f32)
    brow[0, 0:32] = np.asarray(inp["L0_ffn_bias_corr"], f32)
    brow[0, 32:64] = 1e-30 * (np.arange(32) + 1)
    esel = np.zeros((4, 32), f32)
    for e in range(4):
        esel[e, 4 * core + e] = 1.0
    brow[0, 64:192] = esel.reshape(-1)

    frac = np.arange(0, 128, 2, dtype=np.float64) / 128.0
    ts = 10000.0 ** frac

    def rope_tables(tstart, ntiles):
        pos = tstart + np.arange(ntiles * 128).reshape(ntiles, 128)
        pos = np.maximum(pos, 0).astype(np.float64)
        ang = pos[:, :, None] / ts[None, None, :]
        sin = np.sin(ang).astype(ml_dtypes.bfloat16).transpose(1, 0, 2)
        cos = np.cos(ang).astype(ml_dtypes.bfloat16).transpose(1, 0, 2)
        return np.ascontiguousarray(sin), np.ascontiguousarray(cos)

    t0 = 1024 * chunk
    sinq, cosq = rope_tables(t0, 8)
    sink, cosk = rope_tables(t0 - 1024, 16)

    maskin = np.zeros((8, 128, WIN * 128), ml_dtypes.bfloat16)
    iii = np.arange(128)
    for tau in range(8):
        for b in range(WIN):
            blk = np.zeros((128, 128), np.float32)
            if 8 * chunk - 8 + tau + b >= 0:
                if b == 0:
                    blk[:] = (iii[None, :] > iii[:, None])
                elif b == WIN - 1:
                    blk[:] = (iii[None, :] <= iii[:, None])
                else:
                    blk[:] = 1.0
            maskin[tau, :, b * 128:(b + 1) * 128] = blk.astype(ml_dtypes.bfloat16)

    ii = np.arange(128)
    tri_diag = (ii[None, :] <= ii[:, None]).astype(ml_dtypes.bfloat16)
    tri_far = (ii[None, :] > ii[:, None]).astype(ml_dtypes.bfloat16)
    tri_cum = (ii[:, None] < ii[None, :]).astype(ml_dtypes.bfloat16)
    identb = np.eye(128, dtype=ml_dtypes.bfloat16)
    identf = np.eye(128, dtype=f32)
    i4 = (ii[:, None] // 32 == np.arange(4)[None, :])
    return dict(
        x_own=x_own, x_halo=x_halo, wq=wq, wkv=wkv, wo=wo, gate_wT=gate_wT,
        up_wT=up_wT, down_wT=down_wT, sh_upT=sh_upT, sh_dnT=sh_dnT,
        qns=qns, kns=kns, brow_in=brow, sinq=sinq, cosq=cosq, sink=sink, cosk=cosk,
        maskin=maskin, tri_diag=tri_diag, tri_far=tri_far, tri_cum=tri_cum,
        identb=identb, identf=identf,
        i4f=i4.astype(f32), i4b=i4.astype(ml_dtypes.bfloat16),
        ones_f=np.ones((1, 128), f32),
        actc=np.tile(np.array([[1e-6, 1.0 / 128.0, -0.5, np.log(0.08838834764831845),
                                0, 0, 0, 0]], f32), (128, 1)),
        ones_cb=np.ones((128, 1), ml_dtypes.bfloat16),
    )


from concourse.bass_utils import run_bass_kernel_spmd

_NC_CACHE = {}


def kernel(**inputs):
    if "nc" not in _NC_CACHE:
        nc, _ = build_nc(stage=9, debug=0)
        nc.finalize()
        _NC_CACHE["nc"] = nc
    nc = _NC_CACHE["nc"]
    in_maps = [make_core_inputs(inputs, c) for c in range(8)]
    res = run_bass_kernel_spmd(nc, in_maps, core_ids=list(range(8)))
    out = np.concatenate([res.results[c]["out"] for c in range(8)], axis=0)
    return out.reshape(2, 4096, 2048).astype(np.float32)
